# version 07
# baseline (speedup 1.0000x reference)
import numpy as np
import ml_dtypes
import copy
from contextlib import ExitStack

import concourse.bass as bass
import concourse.mybir as mybir
from concourse.bass_utils import run_bass_kernel_spmd

F32 = mybir.dt.float32
BF16 = mybir.dt.bfloat16
AF = mybir.ActivationFunctionType
ALU = mybir.AluOpType
AX = mybir.AxisListType

MASK_NEG = -30000.0
EPS = 1e-5

CFG_FULL = dict(S=4096, D=2048, QL=1536, KVL=512, H=16, F=8192)
CFG_SMALL = dict(S=1024, D=512, QL=256, KVL=256, H=2, F=1024)

ROPE = 64
NOPE = 64
VD = 128
HD = NOPE + ROPE
HKV = NOPE + VD
SEG = 2048

ENGINES = ("pe", "dve", "act", "gp", "sp")


def seg_list(E):
    segs = []
    rem = E
    while rem > SEG:
        segs.append(SEG)
        rem -= SEG
    segs.append(rem)
    return segs


def slab_list(width):
    out = []
    k = 0
    while k < width:
        w = min(512, width - k)
        out.append((k, w))
        k += w
    return out


def swap_view(ap_2d, n):
    return bass.AP(ap_2d.tensor, ap_2d.offset + 1,
                   [list(ap_2d.ap[0]), [2, n // 2], [-1, 2]])


class Ctx:

    def __init__(self, block, handles):
        self.block = block
        self.handles = handles
        self.state = {"sems": {n: 0 for n in handles},
                      "wm": {e: {} for e in ENGINES}}
        self.cur = None
        self.eng = None

    def snap(self):
        return copy.deepcopy(self.state)

    def restore(self, s):
        self.state = copy.deepcopy(s)

    def wait(self, engine, sem, val):
        if val <= 0:
            return
        wm = self.state["wm"][engine]
        if wm.get(sem, 0) >= val:
            return
        wm[sem] = val
        if self.cur == engine:
            self.eng.wait_ge(self.handles[sem], val)

    def waitt(self, engine, tok):
        if tok is not None:
            self.wait(engine, tok[0], tok[1])

    def ins(self, engine, fn, inc=None, k=1):
        if inc is not None:
            self.state["sems"][inc] += k
        if self.cur == engine:
            bi = fn(self.eng)
            if inc is not None:
                bi.then_inc(self.handles[inc], k)
        return (inc, self.state["sems"][inc]) if inc is not None else None

    def tok(self, sem):
        return (sem, self.state["sems"][sem])

    def run_phase(self, phase_fn):
        s0 = self.snap()
        calls = [("pe", self.block.tensor), ("dve", self.block.vector),
                 ("act", self.block.scalar), ("gp", self.block.gpsimd),
                 ("sp", self.block.sync)]
        for name, bc in calls:
            self.restore(s0)

            def body(eng, name=name):
                self.cur = name
                self.eng = eng
                phase_fn(self)
                self.cur = None
                self.eng = None

            bc(body)

    def full_gate(self):
        def g(c):
            for e in ENGINES:
                for sem, v in list(c.state["sems"].items()):
                    c.wait(e, sem, v)
        self.run_phase(g)


def build_program(cfg):
    S, D, QL, KVL, H, F = cfg["S"], cfg["D"], cfg["QL"], cfg["KVL"], cfg["H"], cfg["F"]
    KVOUT = KVL + ROPE
    DC, QLC, KVC, FC = D // 128, QL // 128, KVL // 128, F // 128
    NT, NO = S // 128, S // 256
    TOK = NO * 128
    TH = TOK // 2
    NTH = TH // 128
    SCALE = float(HD) ** -0.5
    SUP = 512
    NSUP = F // SUP
    SUPC = SUP // 128

    nc = bass.Bass("TRN2", target_bir_lowering=False)

    x_d = nc.dram_tensor("x", [S, D], F32, kind="ExternalInput")
    wqa_d = nc.dram_tensor("wqa", [D, QL], F32, kind="ExternalInput")
    wqb_d = nc.dram_tensor("wqb", [QL, H * HD], F32, kind="ExternalInput")
    wkva_d = nc.dram_tensor("wkva", [D, KVOUT], F32, kind="ExternalInput")
    wkvb_d = nc.dram_tensor("wkvb", [KVL, H * HKV], F32, kind="ExternalInput")
    wo_d = nc.dram_tensor("wo", [H * VD, D], F32, kind="ExternalInput")
    wgT_d = nc.dram_tensor("wgT", [D, F], F32, kind="ExternalInput")
    wuT_d = nc.dram_tensor("wuT", [D, F], F32, kind="ExternalInput")
    wdT_d = nc.dram_tensor("wdT", [F, D], F32, kind="ExternalInput")
    idnf_d = nc.dram_tensor("idnf", [128, 128], F32, kind="ExternalInput")
    idnb_d = nc.dram_tensor("idnb", [128, 128], BF16, kind="ExternalInput")
    cosq_d = nc.dram_tensor("cosq", [128, NO * ROPE], F32, kind="ExternalInput")
    sinq_d = nc.dram_tensor("sinq", [128, NO * ROPE], F32, kind="ExternalInput")
    cosk_d = nc.dram_tensor("cosk", [128, NT * ROPE], F32, kind="ExternalInput")
    sink_d = nc.dram_tensor("sink", [128, NT * ROPE], F32, kind="ExternalInput")
    masks_d = nc.dram_tensor("masks", [128, NO * 512], BF16, kind="ExternalInput")
    y_d = nc.dram_tensor("y", [TOK, D], F32, kind="ExternalOutput")
    qlat_sc = nc.dram_tensor("qlat_sc", [NO, 128, QL], BF16, kind="Internal")
    q_sc = nc.dram_tensor("q_sc", [NO, 128, H * HD], BF16, kind="Internal")
    attn_sc = nc.dram_tensor("attn_sc", [H, 128, TOK], BF16, kind="Internal")

    xa = x_d.ap().rearrange("(t r) d -> t r d", r=128)
    ya = y_d.ap().rearrange("(t r) d -> t r d", r=128)
    qlat_a = qlat_sc.ap()
    q_a = q_sc.ap()
    attn_a = attn_sc.ap()

    with ExitStack() as st:
        idn_f = st.enter_context(nc.sbuf_tensor("idn_f", [128, 128], F32))
        idn_b = st.enter_context(nc.sbuf_tensor("idn_b", [128, 128], BF16))
        ones_neg = st.enter_context(nc.sbuf_tensor("ones_neg", [128, 1], BF16))
        ones_inv = st.enter_context(nc.sbuf_tensor("ones_inv", [128, 1], BF16))
        ones_row = st.enter_context(nc.sbuf_tensor("ones_row", [1, 128], F32))
        sm = st.enter_context(nc.sbuf_tensor("sm", [128, 32], F32))
        bnb = st.enter_context(nc.sbuf_tensor("bnb", [128, 24], F32))
        oacc = st.enter_context(nc.sbuf_tensor("oacc", [128, 128], F32))
        epsb = st.enter_context(nc.sbuf_tensor("epsb", [128, 1], F32))

        ps4 = st.enter_context(nc.psum_tensor("ps4", [128, 2048], F32))
        pex = st.enter_context(nc.psum_tensor("pex", [128, 512], F32))
        pvb = st.enter_context(nc.psum_tensor("pvb", [128, 512], F32))
        ptrA = st.enter_context(nc.psum_tensor("ptrA", [128, 1024], BF16))
        ptrB = st.enter_context(nc.psum_tensor("ptrB", [128, 1024], BF16))

        semnames = ["dS", "dG", "dW0", "dW1", "dX0", "dX1", "dQ0", "dQ1", "dL0", "dL1", "dY0", "dY1", "dA0", "dA1", "tP", "tV", "tA", "tG"]
        handles = {n: st.enter_context(nc.semaphore(n)) for n in semnames}
        block = st.enter_context(nc.Block())
        c = Ctx(block, handles)


        def ph_pro(cc):
            cc.ins("sp", lambda e: e.dma_start(out=idn_f[:], in_=idnf_d.ap()), inc="dS", k=16)
            cc.ins("sp", lambda e: e.dma_start(out=idn_b[:], in_=idnb_d.ap()), inc="dS", k=16)
            cc.ins("dve", lambda e: e.memset(ones_neg[:], -1.0 / D), inc="tV")
            cc.ins("dve", lambda e: e.memset(ones_inv[:], 1.0 / D), inc="tV")
            cc.ins("dve", lambda e: e.memset(ones_row[:], 1.0), inc="tV")
            cc.ins("dve", lambda e: e.memset(epsb[:], EPS), inc="tV")

        c.run_phase(ph_pro)
        c.full_gate()

        with ExitStack() as stAB:
            c_kv = stAB.enter_context(nc.sbuf_tensor("c_kv", [128, KVC * S], BF16))
            kpe_bf = stAB.enter_context(nc.sbuf_tensor("kpe_bf", [64, S], BF16))
            wkvb_bf = stAB.enter_context(nc.sbuf_tensor("wkvb_bf", [128, KVC * H * HKV], BF16))

            with ExitStack() as stA:
                cosk = stA.enter_context(nc.sbuf_tensor("cosk_s", [128, NT * ROPE], F32))
                sink = stA.enter_context(nc.sbuf_tensor("sink_s", [128, NT * ROPE], F32))
                xring = stA.enter_context(nc.sbuf_tensor("xring", [128, 2 * D], F32))
                wqa_bf = stA.enter_context(nc.sbuf_tensor("wqa_bf", [128, DC * QL], BF16))
                wkva_bf = stA.enter_context(nc.sbuf_tensor("wkva_bf", [128, DC * KVOUT], BF16))
                wst = stA.enter_context(nc.sbuf_tensor("wst", [128, 2 * 2048], F32))
                xn1tm = stA.enter_context(nc.sbuf_tensor("xn1tm", [128, 2 * D], BF16))
                xn1fm = stA.enter_context(nc.sbuf_tensor("xn1fm", [128, 2 * D], BF16))
                kst = stA.enter_context(nc.sbuf_tensor("kst", [128, 192], F32))
                qlst = stA.enter_context(nc.sbuf_tensor("qlst", [128, 2 * QL], BF16))

                def ph_a1(cc):
                    pieces = []
                    for dcx in range(DC):
                        pieces.append((wkva_d.ap(), dcx, 0, KVOUT, wkva_bf, dcx * KVOUT))
                    for dcx in range(DC):
                        off = 0
                        while off < QL:
                            w = min(2048, QL - off)
                            pieces.append((wqa_d.ap(), dcx, off, w, wqa_bf, dcx * QL + off))
                            off += w
                    for ccx in range(KVC):
                        off = 0
                        while off < H * HKV:
                            w = min(2048, H * HKV - off)
                            pieces.append((wkvb_d.ap(), ccx, off, w, wkvb_bf, ccx * H * HKV + off))
                            off += w
                    npc = len(pieces)
                    cast_tok = [None] * npc
                    kva_last = [None]
                    qa_last = [None]

                    for j in range(npc):
                        src, chk, off, w, dst, doff = pieces[j]
                        slot = j % 2
                        if j >= 2:
                            cc.waitt("sp", cast_tok[j - 2])
                        dw = "dW0" if slot == 0 else "dW1"
                        cc.ins("sp", lambda e, src=src, chk=chk, off=off, w=w, slot=slot:
                               e.dma_start(out=wst[:, slot * 2048: slot * 2048 + w],
                                           in_=src[chk * 128:(chk + 1) * 128, off:off + w]),
                               inc=dw, k=16)
                        cc.wait("gp", dw, cc.state["sems"][dw])
                        tokc = cc.ins("gp", lambda e, slot=slot, w=w, dst=dst, doff=doff:
                                      e.tensor_copy(out=dst[:, doff:doff + w],
                                                    in_=wst[:, slot * 2048: slot * 2048 + w]),
                                      inc="tG")
                        cast_tok[j] = tokc
                        if j == DC - 1:
                            kva_last[0] = tokc
                        if j == npc - KVC * ((H * HKV + 2047) // 2048) - 1:
                            qa_last[0] = tokc

                    for buf, src in ((cosk, cosk_d), (sink, sink_d)):
                        cc.ins("sp", lambda e, buf=buf, src=src:
                               e.dma_start(out=buf[:], in_=src.ap()), inc="dS", k=16)
                    tabs_tok = cc.tok("dS")

                    xr_free = [None, None]
                    tm_free = [None, None]
                    fm_free = [None, None]
                    ql_free = [None, None]
                    ptr_free = [None, None]
                    bank_free = [None, None]
                    pex_free = [None]

                    for t in range(NT):
                        sl = t % 2
                        dx = "dX0" if sl == 0 else "dX1"
                        cc.waitt("gp", xr_free[sl])
                        cc.ins("gp", lambda e, t=t, sl=sl:
                               e.dma_start(out=xring[:, sl * D:(sl + 1) * D], in_=xa[t]),
                               inc=dx, k=16)
                        cc.wait("dve", dx, cc.state["sems"][dx])
                        nch = (D + 511) // 512
                        for j in range(nch):
                            w = min(512, D - j * 512)
                            cc.ins("dve", lambda e, sl=sl, j=j, w=w:
                                   e.bn_stats(bnb[:, j * 6:(j + 1) * 6],
                                              xring[:, sl * D + j * 512: sl * D + j * 512 + w]),
                                   inc="tV" if j == nch - 1 else None)
                        cc.wait("dve", "tV", cc.state["sems"]["tV"])
                        cc.ins("dve", lambda e, nch=nch: e.bn_aggr(sm[:, 0:2], bnb[:, 0:6 * nch]), inc="tV")
                        agg = cc.tok("tV")
                        cc.waitt("act", agg)
                        cc.ins("act", lambda e: e.activation(out=sm[:, 2:3], in_=sm[:, 1:2],
                                                             func=AF.Ln, bias=epsb[:, 0:1]), inc="tA")
                        cc.wait("act", "tA", cc.state["sems"]["tA"])
                        cc.ins("act", lambda e: e.activation(out=sm[:, 2:3], in_=sm[:, 2:3],
                                                             func=AF.Exp, scale=-0.5), inc="tA")
                        rstd = cc.tok("tA")
                        cc.waitt("dve", rstd)
                        cc.waitt("dve", tm_free[sl])
                        cc.ins("dve", lambda e, sl=sl:
                               e.tensor_scalar(out=xn1tm[:, sl * D:(sl + 1) * D],
                                               in0=xring[:, sl * D:(sl + 1) * D],
                                               scalar1=sm[:, 0:1], scalar2=sm[:, 2:3],
                                               op0=ALU.subtract, op1=ALU.mult),
                               inc="tV")
                        norm = cc.tok("tV")
                        xr_free[sl] = norm
                        cc.waitt("pe", norm)
                        cc.waitt("pe", fm_free[sl])
                        for r in range(DC // 4):
                            half = r % 2
                            ptr_t = (ptrA, ptrB)[half]
                            cc.waitt("pe", ptr_free[half])
                            for j in range(4):
                                dcx = r * 4 + j
                                cc.ins("pe", lambda e, sl=sl, dcx=dcx, j=j, ptr_t=ptr_t:
                                       e.transpose(ptr_t[:, j * 128:(j + 1) * 128],
                                                   xn1tm[:, sl * D + dcx * 128: sl * D + (dcx + 1) * 128],
                                                   idn_b[:]),
                                       inc="tP" if j == 3 else None)
                            tr = cc.tok("tP")
                            cc.waitt("dve", tr)
                            cc.ins("dve", lambda e, sl=sl, r=r, ptr_t=ptr_t:
                                   e.tensor_copy(out=xn1fm[:, sl * D + r * 512: sl * D + (r + 1) * 512],
                                                 in_=ptr_t[:, 0:512]),
                                   inc="tV")
                            ptr_free[half] = cc.tok("tV")
                        tm_free[sl] = cc.tok("tP")
                        fm_ready = cc.tok("tV")

                        cc.waitt("pe", fm_ready)
                        cc.waitt("pe", kva_last[0])
                        oc_list = [("kva", o) for o in range(KVC + 1)]
                        if t % 2 == 0:
                            oc_list += [("qa", o) for o in range(QLC)]
                        for idx, (kind, oc) in enumerate(oc_list):
                            if kind == "qa":
                                cc.waitt("pe", qa_last[0])
                            bk = idx % 2
                            bko = bk * 512
                            cc.waitt("pe", bank_free[bk])
                            is_kpe = (kind == "kva" and oc == KVC)
                            m = 64 if is_kpe else 128
                            for dcx in range(DC):
                                if kind == "kva":
                                    lo = dcx * KVOUT + oc * 128
                                    lhs = wkva_bf
                                else:
                                    lo = dcx * QL + oc * 128
                                    lhs = wqa_bf
                                cc.ins("pe", lambda e, bko=bko, lhs=lhs, lo=lo, m=m, sl=sl, dcx=dcx,
                                       first=(dcx == 0), last=(dcx == DC - 1):
                                       e.matmul(ps4[0:m, bko:bko + 128],
                                                lhs[:, lo:lo + m],
                                                xn1fm[:, sl * D + dcx * 128: sl * D + (dcx + 1) * 128],
                                                start=first, stop=last),
                                       inc="tP" if dcx == DC - 1 else None)
                            grp = cc.tok("tP")
                            if kind == "kva" and oc < KVC:
                                cc.waitt("act", grp)
                                cc.ins("act", lambda e, bko=bko, oc=oc, t=t:
                                       e.activation(out=c_kv[:, oc * S + t * 128: oc * S + (t + 1) * 128],
                                                    in_=ps4[:, bko:bko + 128], func=AF.Copy),
                                       inc="tA")
                                bank_free[bk] = cc.tok("tA")
                            elif is_kpe:
                                cc.waitt("act", grp)
                                cc.ins("act", lambda e, bko=bko:
                                       e.activation(out=kst[0:64, 0:128],
                                                    in_=ps4[0:64, bko:bko + 128], func=AF.Copy),
                                       inc="tA")
                                kc = cc.tok("tA")
                                bank_free[bk] = kc
                                cc.waitt("pe", kc)
                                cc.waitt("pe", pex_free[0])
                                cc.ins("pe", lambda e: e.transpose(pex[:, 0:64], kst[0:64, 0:128], idn_f[0:64, 0:64]),
                                       inc="tP")
                                ktr = cc.tok("tP")
                                cc.waitt("dve", ktr)
                                cc.waitt("dve", tabs_tok)
                                cc.ins("dve", lambda e, t=t:
                                       e.tensor_tensor(out=kst[:, 64:128], in0=pex[:, 0:64],
                                                       in1=cosk[:, t * ROPE:(t + 1) * ROPE], op=ALU.mult),
                                       inc="tV")
                                cc.ins("dve", lambda e, t=t:
                                       e.tensor_tensor(out=kst[:, 128:192],
                                                       in0=swap_view(pex[:, 0:64], 64),
                                                       in1=sink[:, t * ROPE:(t + 1) * ROPE], op=ALU.mult),
                                       inc="tV")
                                cc.wait("dve", "tV", cc.state["sems"]["tV"])
                                cc.ins("dve", lambda e:
                                       e.tensor_tensor(out=kst[:, 64:128], in0=kst[:, 64:128],
                                                       in1=kst[:, 128:192], op=ALU.add),
                                       inc="tV")
                                krp = cc.tok("tV")
                                cc.waitt("pe", krp)
                                cc.ins("pe", lambda e: e.transpose(pex[0:64, 128:256], kst[:, 64:128], idn_f[:]),
                                       inc="tP")
                                ktr2 = cc.tok("tP")
                                cc.waitt("act", ktr2)
                                cc.ins("act", lambda e, t=t:
                                       e.activation(out=kpe_bf[0:64, t * 128:(t + 1) * 128],
                                                    in_=pex[0:64, 128:256], func=AF.Copy),
                                       inc="tA")
                                pex_free[0] = cc.tok("tA")
                            else:
                                qsl = (t // 2) % 2
                                cc.waitt("dve", grp)
                                if oc == 0:
                                    cc.waitt("dve", ql_free[qsl])
                                cc.ins("dve", lambda e, bko=bko, oc=oc, qsl=qsl:
                                       e.tensor_copy(out=qlst[:, qsl * QL + oc * 128: qsl * QL + (oc + 1) * 128],
                                                     in_=ps4[:, bko:bko + 128]),
                                       inc="tV")
                                bank_free[bk] = cc.tok("tV")
                                if oc == QLC - 1:
                                    qev = cc.tok("tV")
                                    cc.waitt("sp", qev)
                                    dl = "dL0" if qsl == 0 else "dL1"
                                    cc.ins("sp", lambda e, t=t, qsl=qsl:
                                           e.dma_start(out=qlat_a[t // 2],
                                                       in_=qlst[:, qsl * QL:(qsl + 1) * QL]),
                                           inc=dl, k=16)
                                    ql_free[qsl] = cc.tok(dl)
                        fm_free[sl] = cc.tok("tP")

                c.run_phase(ph_a1)
            c.full_gate()

            with ExitStack() as stA2:
                cosq = stA2.enter_context(nc.sbuf_tensor("cosq2", [128, NO * ROPE], F32))
                sinq = stA2.enter_context(nc.sbuf_tensor("sinq2", [128, NO * ROPE], F32))
                wqb_bf = stA2.enter_context(nc.sbuf_tensor("wqb_bf", [128, QLC * H * HD], BF16))
                wst2 = stA2.enter_context(nc.sbuf_tensor("wst2", [128, 2 * 2048], F32))
                qlring = stA2.enter_context(nc.sbuf_tensor("qlring", [128, 2 * QL], BF16))
                qstage = stA2.enter_context(nc.sbuf_tensor("qstage", [128, 2 * H * HD], BF16))

                def ph_a2(cc):
                    for buf, src in ((cosq, cosq_d), (sinq, sinq_d)):
                        cc.ins("sp", lambda e, buf=buf, src=src:
                               e.dma_start(out=buf[:], in_=src.ap()), inc="dS", k=16)
                    tabs = cc.tok("dS")
                    pieces = []
                    for lcx in range(QLC):
                        off = 0
                        while off < H * HD:
                            w = min(2048, H * HD - off)
                            pieces.append((lcx, off, w))
                            off += w
                    cast_tok = [None] * len(pieces)
                    wqb_last = [None]
                    pi = [0]

                    def emit_pieces(n):
                        j = pi[0]
                        for _ in range(n):
                            if j >= len(pieces):
                                break
                            lcx, off, w = pieces[j]
                            slot = j % 2
                            if j >= 2:
                                cc.waitt("sp", cast_tok[j - 2])
                            dw = "dW0" if slot == 0 else "dW1"
                            cc.ins("sp", lambda e, lcx=lcx, off=off, w=w, slot=slot:
                                   e.dma_start(out=wst2[:, slot * 2048: slot * 2048 + w],
                                               in_=wqb_d.ap()[lcx * 128:(lcx + 1) * 128, off:off + w]),
                                   inc=dw, k=16)
                            cc.wait("gp", dw, cc.state["sems"][dw])
                            tokc = cc.ins("gp", lambda e, slot=slot, w=w, lcx=lcx, off=off:
                                          e.tensor_copy(out=wqb_bf[:, lcx * H * HD + off: lcx * H * HD + off + w],
                                                        in_=wst2[:, slot * 2048: slot * 2048 + w]),
                                          inc="tG")
                            cast_tok[j] = tokc
                            if j == len(pieces) - 1:
                                wqb_last[0] = tokc
                            j += 1
                        pi[0] = j

                    emit_pieces(len(pieces))

                    qr_free = [None, None]
                    qs_free = [None, None]
                    nslab = (H * HD + 511) // 512
                    bank_free = [None] * 4

                    for i in range(NO):
                        sl = i % 2
                        dq = "dQ0" if sl == 0 else "dQ1"
                        cc.waitt("gp", qr_free[sl])
                        cc.ins("gp", lambda e, i=i, sl=sl:
                               e.dma_start(out=qlring[:, sl * QL:(sl + 1) * QL], in_=qlat_a[i]),
                               inc=dq, k=16)
                        qload = cc.tok(dq)
                        cc.waitt("pe", qload)
                        cc.waitt("pe", wqb_last[0])
                        if i == 0:
                            pass
                        for s in range(nslab):
                            w = min(512, H * HD - s * 512)
                            bk = s % 4
                            cc.waitt("pe", bank_free[bk])
                            for lcx in range(QLC):
                                cc.ins("pe", lambda e, bk=bk, w=w, lcx=lcx, s=s, sl=sl,
                                       first=(lcx == 0), last=(lcx == QLC - 1):
                                       e.matmul(ps4[:, bk * 512: bk * 512 + w],
                                                qlring[:, sl * QL + lcx * 128: sl * QL + (lcx + 1) * 128],
                                                wqb_bf[:, lcx * H * HD + s * 512: lcx * H * HD + s * 512 + w],
                                                start=first, stop=last),
                                       inc="tP" if lcx == QLC - 1 else None)
                            grp = cc.tok("tP")
                            cc.waitt("act", grp)
                            if s == 0:
                                cc.waitt("act", qs_free[sl])
                            cc.ins("act", lambda e, bk=bk, w=w, s=s, sl=sl:
                                   e.activation(out=qstage[:, sl * H * HD + s * 512: sl * H * HD + s * 512 + w],
                                                in_=ps4[:, bk * 512: bk * 512 + w], func=AF.Copy),
                                   inc="tA")
                            ev = cc.tok("tA")
                            cc.waitt("dve", grp)
                            cc.waitt("dve", ev)
                            cc.wait("dve", "dS", tabs[1])
                            for hh in range(w // HD):
                                pecol = bk * 512 + hh * HD + NOPE
                                qcol = sl * H * HD + s * 512 + hh * HD + NOPE
                                cc.wait("dve", "tV", cc.state["sems"]["tV"])
                                cc.ins("dve", lambda e, pecol=pecol, i=i:
                                       e.tensor_tensor(out=kst_a2[:, 0:64],
                                                       in0=ps4[:, pecol:pecol + 64],
                                                       in1=cosq[:, i * ROPE:(i + 1) * ROPE], op=ALU.mult),
                                       inc="tV")
                                cc.ins("dve", lambda e, pecol=pecol, i=i:
                                       e.tensor_tensor(out=kst_a2[:, 64:128],
                                                       in0=swap_view(ps4[:, pecol:pecol + 64], 64),
                                                       in1=sinq[:, i * ROPE:(i + 1) * ROPE], op=ALU.mult),
                                       inc="tV")
                                cc.wait("dve", "tV", cc.state["sems"]["tV"])
                                cc.ins("dve", lambda e, qcol=qcol:
                                       e.tensor_tensor(out=qstage[:, qcol:qcol + 64],
                                                       in0=kst_a2[:, 0:64], in1=kst_a2[:, 64:128],
                                                       op=ALU.add),
                                       inc="tV")
                            bank_free[bk] = cc.tok("tV")
                        qr_free[sl] = cc.tok("tP")
                        done = cc.tok("tV")
                        cc.waitt("sp", done)
                        cc.wait("sp", "tA", cc.state["sems"]["tA"])
                        dl = "dL0" if sl == 0 else "dL1"
                        cc.ins("sp", lambda e, i=i, sl=sl:
                               e.dma_start(out=q_a[i], in_=qstage[:, sl * H * HD:(sl + 1) * H * HD]),
                               inc=dl, k=16)
                        qs_free[sl] = cc.tok(dl)

                kst_a2 = stA2.enter_context(nc.sbuf_tensor("kst_a2", [128, 128], F32))
                c.run_phase(ph_a2)
            c.full_gate()

            with ExitStack() as stB:
                masks_sb = stB.enter_context(nc.sbuf_tensor("masks_sb", [128, NO * 512], BF16))
                kfull = stB.enter_context(nc.sbuf_tensor("kfull", [128, S], BF16))
                v_tm = stB.enter_context(nc.sbuf_tensor("v_tm", [128, S], BF16))
                vslab = stB.enter_context(nc.sbuf_tensor("vslab", [128, 2 * 512], BF16))
                qh = stB.enter_context(nc.sbuf_tensor("qh", [128, 2 * NO * 128], BF16))
                qTb = stB.enter_context(nc.sbuf_tensor("qTb", [128, 2 * 128], BF16))
                Pbuf = stB.enter_context(nc.sbuf_tensor("Pbuf", [128, 2 * 2048], BF16))
                PT = stB.enter_context(nc.sbuf_tensor("PT", [128, 4 * 512], BF16))
                obf = stB.enter_context(nc.sbuf_tensor("obf", [128, 2 * 128], BF16))
                astage = stB.enter_context(nc.sbuf_tensor("astage", [128, 2 * TOK], BF16))

                def ph_b(cc):
                    cc.ins("sp", lambda e: e.dma_start(out=masks_sb[:], in_=masks_d.ap()), inc="dS", k=16)
                    masks_tok = cc.tok("dS")
                    cc.ins("dve", lambda e: e.tensor_copy(out=kfull[64:128, :], in_=kpe_bf[:]), inc="tV")

                    qh_free = [None, None]
                    qh_toks = [None, None]
                    ast_free = [None, None]
                    kfull_free = None
                    vtm_free = None
                    pex_free = None
                    pv_free = None
                    acc_free = [None, None]
                    ptr_free = [None, None]
                    vs_free = [None, None]
                    ps_free = [None] * 4
                    qT_free = [None, None]
                    P_free = [None] * 2
                    PT_free = [None] * 4
                    l_free = {}
                    obf_free = [None, None]
                    ladd_prev = [None]
                    negm_free = {}
                    o_war = [None]

                    NKS = S // 512
                    ACCB = (pvb, pex)

                    def qh_load(h):
                        hsl = h % 2
                        dq = "dQ0" if hsl == 0 else "dQ1"
                        cc.waitt("gp", qh_free[hsl])
                        qsrc = bass.AP(q_sc, h * HD,
                                       [[H * HD, 128], [128 * H * HD, NO], [1, 128]])
                        cc.ins("gp", lambda e, hsl=hsl, qsrc=qsrc:
                               e.dma_start(out=bass.AP(qh, hsl * NO * 128,
                                                       [[2 * NO * 128, 128], [128, NO], [1, 128]]),
                                           in_=qsrc),
                               inc=dq, k=16)
                        qh_toks[hsl] = cc.tok(dq)

                    for h in range(H):
                        hsl = h % 2
                        if h == 0:
                            qh_load(0)

                        cc.waitt("pe", kfull_free)
                        cc.waitt("pe", vtm_free)
                        for ks in range(NKS):
                            cc.waitt("pe", pex_free)
                            cc.waitt("pe", acc_free[1])
                            for ccx in range(KVC):
                                lo = ccx * H * HKV + h * HKV
                                cc.ins("pe", lambda e, ccx=ccx, lo=lo, ks=ks,
                                       first=(ccx == 0):
                                       e.matmul(pex[0:64, 0:512],
                                                wkvb_bf[:, lo:lo + 64],
                                                c_kv[:, ccx * S + ks * 512: ccx * S + (ks + 1) * 512],
                                                start=first, stop=(ccx == KVC - 1)),
                                       inc="tP" if ccx == KVC - 1 else None)
                            kgrp = cc.tok("tP")
                            cc.waitt("act", kgrp)
                            cc.ins("act", lambda e, ks=ks:
                                   e.activation(out=kfull[0:64, ks * 512:(ks + 1) * 512],
                                                in_=pex[0:64, 0:512], func=AF.Copy, scale=SCALE),
                                   inc="tA")
                            pex_free = cc.tok("tA")
                            acc_free[1] = pex_free
                            cc.waitt("pe", pv_free)
                            cc.waitt("pe", acc_free[0])
                            for ccx in range(KVC):
                                lo = ccx * H * HKV + h * HKV + NOPE
                                cc.ins("pe", lambda e, ccx=ccx, lo=lo, ks=ks,
                                       first=(ccx == 0):
                                       e.matmul(pvb[:, 0:512],
                                                wkvb_bf[:, lo:lo + VD],
                                                c_kv[:, ccx * S + ks * 512: ccx * S + (ks + 1) * 512],
                                                start=first, stop=(ccx == KVC - 1)),
                                       inc="tP" if ccx == KVC - 1 else None)
                            vgrp = cc.tok("tP")
                            vsl = ks % 2
                            cc.waitt("dve", vgrp)
                            cc.waitt("dve", vs_free[vsl])
                            cc.ins("dve", lambda e, vsl=vsl:
                                   e.tensor_copy(out=vslab[:, vsl * 512:(vsl + 1) * 512], in_=pvb[:, 0:512]),
                                   inc="tV")
                            vev = cc.tok("tV")
                            pv_free = vev
                            acc_free[0] = vev
                            half = ks % 2
                            ptr_t = (ptrA, ptrB)[half]
                            cc.waitt("pe", vev)
                            cc.waitt("pe", ptr_free[half])
                            for j in range(4):
                                cc.ins("pe", lambda e, vsl=vsl, j=j, ptr_t=ptr_t:
                                       e.transpose(ptr_t[:, j * 128:(j + 1) * 128],
                                                   vslab[:, vsl * 512 + j * 128: vsl * 512 + (j + 1) * 128],
                                                   idn_b[:]),
                                       inc="tP" if j == 3 else None)
                            vtr = cc.tok("tP")
                            vs_free[vsl] = vtr
                            cc.waitt("dve", vtr)
                            cc.ins("dve", lambda e, ks=ks, ptr_t=ptr_t:
                                   e.tensor_copy(out=v_tm[:, ks * 512:(ks + 1) * 512], in_=ptr_t[:, 0:512]),
                                   inc="tV")
                            ptr_free[half] = cc.tok("tV")
                        kv_ready = cc.tok("tV")
                        cc.wait("pe", "tA", cc.state["sems"]["tA"])

                        if h + 1 < H:
                            qh_load(h + 1)
                        qh_tok = qh_toks[hsl]

                        jobs = []
                        for i in range(NO):
                            E = 256 * (i + 1)
                            segs = seg_list(E)
                            sk0 = 0
                            for si, sw in enumerate(segs):
                                jobs.append(dict(i=i, si=si, sk0=sk0, sw=sw,
                                                 nseg=len(segs)))
                                sk0 += sw

                        def stage_S(jidx):
                            jb = jobs[jidx]
                            i, si, sk0, sw = jb["i"], jb["si"], jb["sk0"], jb["sw"]
                            jp = jidx % 2
                            par = i % 2
                            slabs = slab_list(sw)
                            if si == 0:
                                halfq = par
                                ptr_t = (ptrA, ptrB)[halfq]
                                cc.waitt("pe", qh_tok)
                                cc.waitt("pe", ptr_free[halfq])
                                cc.ins("pe", lambda e, hsl=hsl, i=i, ptr_t=ptr_t:
                                       e.transpose(ptr_t[:, 0:128],
                                                   qh[:, hsl * NO * 128 + i * 128: hsl * NO * 128 + (i + 1) * 128],
                                                   idn_b[:]),
                                       inc="tP")
                                qtr = cc.tok("tP")
                                cc.waitt("dve", qtr)
                                cc.waitt("dve", qT_free[par])
                                cc.ins("dve", lambda e, ptr_t=ptr_t, par=par:
                                       e.tensor_copy(out=qTb[:, par * 128:(par + 1) * 128],
                                                     in_=ptr_t[:, 0:128]), inc="tV")
                                jb["qT_tok"] = cc.tok("tV")
                                ptr_free[halfq] = jb["qT_tok"]
                            else:
                                jb["qT_tok"] = jobs[jidx - 1]["qT_tok"]
                            cc.waitt("pe", jb["qT_tok"])
                            cc.waitt("pe", kv_ready)
                            for (k0, w) in slabs:
                                bk = k0 // 512
                                cc.waitt("pe", ps_free[bk])
                                cc.ins("pe", lambda e, bk=bk, w=w, k0=k0, sk0=sk0, par=par:
                                       e.matmul(ps4[:, bk * 512: bk * 512 + w],
                                                qTb[:, par * 128:(par + 1) * 128],
                                                kfull[:, sk0 + k0: sk0 + k0 + w],
                                                start=True, stop=True),
                                       inc="tP")
                            sc_tok = cc.tok("tP")
                            qT_free[par] = sc_tok
                            if si == jb["nseg"] - 1:
                                k0, w = slabs[-1]
                                bk = k0 // 512
                                cc.waitt("dve", sc_tok)
                                cc.waitt("dve", masks_tok)
                                cc.ins("dve", lambda e, bk=bk, w=w, i=i:
                                       e.tensor_tensor(out=ps4[:, bk * 512: bk * 512 + w],
                                                       in0=ps4[:, bk * 512: bk * 512 + w],
                                                       in1=masks_sb[:, i * 512 + 512 - w:(i + 1) * 512],
                                                       op=ALU.add),
                                       inc="tV")
                                sc_tok = cc.tok("tV")
                            cc.waitt("dve", sc_tok)
                            ncol = 3 + (jidx % 4)
                            for tk in negm_free.get(ncol, []):
                                cc.waitt("dve", tk)
                            cc.ins("dve", lambda e, sw=sw, ncol=ncol:
                                   e.reduce_max(sm[:, ncol:ncol + 1], ps4[:, 0:sw],
                                                axis=AX.X, negate=True),
                                   inc="tV")
                            jb["negm"] = cc.tok("tV")
                            jb["ncol"] = ncol
                            jb["jp"] = jp

                        def stage_R1(jidx):
                            jb = jobs[jidx]
                            si, sw = jb["si"], jb["sw"]
                            jp = jb["jp"]
                            ncol = jb["ncol"]
                            lcol = 7 + (jidx % 4)
                            jb["lcol"] = lcol
                            psl = jp
                            cc.waitt("act", jb["negm"])
                            cc.waitt("act", P_free[psl])
                            for tk in l_free.get(lcol, []):
                                cc.waitt("act", tk)
                            cc.ins("act", lambda e, sw=sw, psl=psl, ncol=ncol, lcol=lcol:
                                   e.activation(out=Pbuf[:, psl * 2048: psl * 2048 + sw],
                                                in_=ps4[:, 0:sw],
                                                func=AF.Exp, bias=sm[:, ncol:ncol + 1],
                                                accum_out=sm[:, lcol:lcol + 1]),
                                   inc="tA")
                            expd = cc.tok("tA")
                            for bk in range((sw + 511) // 512):
                                ps_free[bk] = expd
                            jb["expd"] = expd
                            jb["psl"] = psl
                            jb["lfin"] = expd
                            negm_free[ncol] = [expd]
                        def stage_R2(jidx):
                            jb = jobs[jidx]
                            i, si, sk0, sw = jb["i"], jb["si"], jb["sk0"], jb["sw"]
                            jp = jb["jp"]
                            par = i % 2
                            ncol = jb["ncol"]
                            accb = ACCB[jp]
                            expd = jb["expd"]
                            psl = jb["psl"]
                            nrounds = (sw + 511) // 512
                            first = True
                            for r in range(nrounds):
                                w = min(512, sw - r * 512)
                                nch = w // 128
                                half2 = r % 2
                                ptr_t2 = (ptrA, ptrB)[half2]
                                cc.waitt("pe", expd)
                                cc.waitt("pe", ptr_free[half2])
                                for j in range(nch):
                                    cc.ins("pe", lambda e, psl=psl, r=r, j=j, ptr_t2=ptr_t2:
                                           e.transpose(ptr_t2[:, j * 128:(j + 1) * 128],
                                                       Pbuf[:, psl * 2048 + r * 512 + j * 128: psl * 2048 + r * 512 + (j + 1) * 128],
                                                       idn_b[:]),
                                           inc="tP" if j == nch - 1 else None)
                                pttr = cc.tok("tP")
                                cc.waitt("dve", pttr)
                                cc.waitt("dve", PT_free[psl * 2 + half2])
                                cc.ins("dve", lambda e, psl=psl, half2=half2, w=w, ptr_t2=ptr_t2:
                                       e.tensor_copy(out=PT[:, (psl * 2 + half2) * 512: (psl * 2 + half2) * 512 + w],
                                                     in_=ptr_t2[:, 0:w]),
                                       inc="tV")
                                ptev = cc.tok("tV")
                                ptr_free[half2] = ptev
                                cc.waitt("pe", ptev)
                                if first:
                                    cc.waitt("pe", acc_free[jp])
                                for j in range(nch):
                                    kt = (sk0 + r * 512) // 128 + j
                                    cc.ins("pe", lambda e, psl=psl, half2=half2, j=j, kt=kt, accb=accb,
                                           fst=(first and j == 0),
                                           lst=(r == nrounds - 1 and j == nch - 1):
                                           e.matmul(accb[:, 0:128],
                                                    PT[:, (psl * 2 + half2) * 512 + j * 128: (psl * 2 + half2) * 512 + (j + 1) * 128],
                                                    v_tm[:, kt * 128:(kt + 1) * 128],
                                                    start=fst, stop=lst),
                                           inc="tP" if (r == nrounds - 1 and j == nch - 1) else None)
                                first = False
                                PT_free[psl * 2 + half2] = cc.tok("tP")
                            av_tok = cc.tok("tP")
                            P_free[psl] = av_tok

                            lcol = jb["lcol"]
                            if jb["nseg"] == 1:
                                cc.wait("dve", "tV", cc.state["sems"]["tV"])
                                cc.waitt("dve", jb["lfin"])
                                cc.waitt("dve", o_war[0])
                                cc.ins("dve", lambda e, lcol=lcol:
                                       e.reciprocal(sm[:, 15:16], sm[:, lcol:lcol + 1]), inc="tV")
                                rec = cc.tok("tV")
                                cc.waitt("act", av_tok)
                                cc.waitt("act", rec)
                                cc.waitt("act", obf_free[par])
                                cc.ins("act", lambda e, accb=accb, par=par:
                                       e.activation(out=obf[:, par * 128:(par + 1) * 128],
                                                    in_=accb[:, 0:128],
                                                    func=AF.Copy, scale=sm[:, 15:16]),
                                       inc="tA")
                                o_tok = cc.tok("tA")
                                o_war[0] = o_tok
                                acc_free[jp] = o_tok
                                l_free[lcol] = [rec]
                            else:
                                if si == 0:
                                    cc.waitt("act", av_tok)
                                    cc.waitt("act", jb.get("oacc_war"))
                                    cc.ins("act", lambda e, accb=accb:
                                           e.activation(out=oacc[:], in_=accb[:, 0:128], func=AF.Copy),
                                           inc="tA")
                                    acc_free[jp] = cc.tok("tA")
                                    jb["ncol0"] = ncol
                                    return
                                ncol0 = jobs[jidx - 1]["ncol0"]
                                l0col = jobs[jidx - 1]["lcol"]
                                cc.wait("dve", "tV", cc.state["sems"]["tV"])
                                cc.wait("dve", "tA", cc.state["sems"]["tA"])
                                cc.ins("dve", lambda e, ncol0=ncol0, ncol=ncol:
                                       e.tensor_tensor(out=sm[:, 17:18], in0=sm[:, ncol0:ncol0 + 1],
                                                       in1=sm[:, ncol:ncol + 1], op=ALU.min), inc="tV")
                                cc.wait("dve", "tV", cc.state["sems"]["tV"])
                                cc.ins("dve", lambda e, ncol0=ncol0:
                                       e.tensor_scalar_mul(sm[:, 11:12], sm[:, ncol0:ncol0 + 1], -1.0), inc="tV")
                                cc.ins("dve", lambda e, ncol=ncol:
                                       e.tensor_scalar_mul(sm[:, 12:13], sm[:, ncol:ncol + 1], -1.0), inc="tV")
                                mm_tok = cc.tok("tV")
                                negm_free.setdefault(ncol0, []).append(mm_tok)
                                negm_free.setdefault(ncol, []).append(mm_tok)
                                cc.waitt("act", mm_tok)
                                cc.ins("act", lambda e:
                                       e.activation(out=sm[:, 13:14], in_=sm[:, 17:18],
                                                    func=AF.Exp, bias=sm[:, 11:12]), inc="tA")
                                cc.ins("act", lambda e:
                                       e.activation(out=sm[:, 14:15], in_=sm[:, 17:18],
                                                    func=AF.Exp, bias=sm[:, 12:13]), inc="tA")
                                cexp = cc.tok("tA")
                                cc.waitt("dve", cexp)
                                cc.ins("dve", lambda e, l0col=l0col:
                                       e.tensor_scalar(out=sm[:, l0col:l0col + 1], in0=sm[:, l0col:l0col + 1],
                                                       scalar1=sm[:, 13:14], scalar2=None,
                                                       op0=ALU.mult), inc="tV")
                                cc.wait("dve", "tV", cc.state["sems"]["tV"])
                                cc.ins("dve", lambda e, l0col=l0col, lcol=lcol:
                                       e.scalar_tensor_tensor(out=sm[:, l0col:l0col + 1], in0=sm[:, lcol:lcol + 1],
                                                              scalar=sm[:, 14:15], in1=sm[:, l0col:l0col + 1],
                                                              op0=ALU.mult, op1=ALU.add), inc="tV")
                                cc.ins("dve", lambda e:
                                       e.tensor_scalar(out=oacc[:], in0=oacc[:],
                                                       scalar1=sm[:, 13:14], scalar2=None,
                                                       op0=ALU.mult), inc="tV")
                                cc.wait("dve", "tV", cc.state["sems"]["tV"])
                                cc.waitt("dve", av_tok)
                                cc.ins("dve", lambda e, accb=accb:
                                       e.scalar_tensor_tensor(out=oacc[:], in0=accb[:, 0:128],
                                                              scalar=sm[:, 14:15], in1=oacc[:],
                                                              op0=ALU.mult, op1=ALU.add), inc="tV")
                                comb = cc.tok("tV")
                                acc_free[jp] = comb
                                cc.wait("dve", "tV", cc.state["sems"]["tV"])
                                cc.waitt("dve", o_war[0])
                                cc.ins("dve", lambda e, l0col=l0col:
                                       e.reciprocal(sm[:, 15:16], sm[:, l0col:l0col + 1]), inc="tV")
                                rec = cc.tok("tV")
                                l_free[l0col] = [rec]
                                l_free[lcol] = [rec]
                                cc.waitt("act", rec)
                                cc.waitt("act", obf_free[par])
                                cc.ins("act", lambda e, par=par:
                                       e.activation(out=obf[:, par * 128:(par + 1) * 128],
                                                    in_=oacc[:],
                                                    func=AF.Copy, scale=sm[:, 15:16]),
                                       inc="tA")
                                o_tok = cc.tok("tA")
                                o_war[0] = o_tok
                            ow = cc.tok("tV")
                            for jj in range(jidx + 1, len(jobs)):
                                if jobs[jj]["si"] == 0 and jobs[jj]["nseg"] > 1:
                                    jobs[jj]["oacc_war"] = ow
                                    break
                            half3 = par
                            ptr_t3 = (ptrA, ptrB)[half3]
                            cc.waitt("pe", o_tok)
                            cc.waitt("pe", ptr_free[half3])
                            cc.ins("pe", lambda e, ptr_t3=ptr_t3, par=par:
                                   e.transpose(ptr_t3[:, 0:128], obf[:, par * 128:(par + 1) * 128],
                                               idn_b[:]), inc="tP")
                            otr = cc.tok("tP")
                            obf_free[par] = otr
                            cc.waitt("dve", otr)
                            cc.waitt("dve", ast_free[hsl])
                            cc.ins("dve", lambda e, hsl=hsl, i=i, ptr_t3=ptr_t3:
                                   e.tensor_copy(out=astage[:, hsl * TOK + i * 128: hsl * TOK + (i + 1) * 128],
                                                 in_=ptr_t3[:, 0:128]),
                                   inc="tV")
                            ptr_free[half3] = cc.tok("tV")

                        for jidx in range(len(jobs)):
                            stage_S(jidx)
                            stage_R1(jidx)
                            if jidx > 0:
                                stage_R2(jidx - 1)
                        stage_R2(len(jobs) - 1)

                        kfull_free = cc.tok("tP")
                        vtm_free = cc.tok("tP")
                        qh_free[hsl] = cc.tok("tP")
                        fin = cc.tok("tV")
                        cc.waitt("gp", fin)
                        da = "dA0" if hsl == 0 else "dA1"
                        cc.ins("gp", lambda e, h=h, hsl=hsl:
                               e.dma_start(out=attn_a[h], in_=astage[:, hsl * TOK:(hsl + 1) * TOK]),
                               inc=da, k=16)
                        ast_free[hsl] = cc.tok(da)

                c.run_phase(ph_b)
            c.full_gate()

        for hf in range(2):
            with ExitStack() as stCD:
                x2 = stCD.enter_context(nc.sbuf_tensor(f"x2_{hf}", [128, DC * TH], F32))

                with ExitStack() as stC:
                    wo_bf = stC.enter_context(nc.sbuf_tensor(f"wo_bf{hf}", [128, (H * VD // 128) * D], BF16))
                    wstC = stC.enter_context(nc.sbuf_tensor(f"wstC{hf}", [128, 2 * 2048], F32))
                    attn_res = stC.enter_context(nc.sbuf_tensor(f"attn_res{hf}", [128, H * TH], BF16))
                    xringC = stC.enter_context(nc.sbuf_tensor(f"xringC{hf}", [128, 2 * D], F32))

                    ICC = H * VD // 128

                    def ph_c(cc, hf=hf):
                        xr_free = [None, None]
                        bank_free = [None, None]
                        ps_free = [None, None]
                        for ii in range(NTH):
                            i = hf * NTH + ii
                            sl = ii % 2
                            dx = "dX0" if sl == 0 else "dX1"
                            cc.waitt("gp", xr_free[sl])
                            cc.ins("gp", lambda e, i=i, sl=sl:
                                   e.dma_start(out=xringC[:, sl * D:(sl + 1) * D], in_=xa[2 * i]),
                                   inc=dx, k=16)
                            xl = cc.tok(dx)
                            cc.waitt("pe", xl)
                            for r in range(DC // 4):
                                pb = (pex, pvb)[r % 2]
                                cc.waitt("pe", bank_free[r % 2])
                                for j in range(4):
                                    dcx = r * 4 + j
                                    cc.ins("pe", lambda e, sl=sl, dcx=dcx, j=j, pb=pb:
                                           e.transpose(pb[:, j * 128:(j + 1) * 128],
                                                       xringC[:, sl * D + dcx * 128: sl * D + (dcx + 1) * 128],
                                                       idn_f[:]),
                                           inc="tP" if j == 3 else None)
                                tr = cc.tok("tP")
                                cc.waitt("dve", tr)
                                for j in range(4):
                                    dcx = r * 4 + j
                                    cc.ins("dve", lambda e, dcx=dcx, ii=ii, j=j, pb=pb:
                                           e.tensor_copy(out=x2[:, dcx * TH + ii * 128: dcx * TH + (ii + 1) * 128],
                                                         in_=pb[:, j * 128:(j + 1) * 128]),
                                           inc="tV" if j == 3 else None)
                                bank_free[r % 2] = cc.tok("tV")
                            xr_free[sl] = cc.tok("tV")
                        x2_ready = cc.tok("tV")

                        cast_tok = [None] * ICC
                        for icx in range(ICC):
                            slot = icx % 2
                            if icx >= 2:
                                cc.waitt("sp", cast_tok[icx - 2])
                            dw = "dW0" if slot == 0 else "dW1"
                            cc.ins("sp", lambda e, icx=icx, slot=slot:
                                   e.dma_start(out=wstC[:, slot * 2048: slot * 2048 + D],
                                               in_=wo_d.ap()[icx * 128:(icx + 1) * 128, :]),
                                   inc=dw, k=16)
                            cc.wait("dve", dw, cc.state["sems"][dw])
                            cast_tok[icx] = cc.ins("dve", lambda e, icx=icx, slot=slot:
                                                   e.tensor_copy(out=wo_bf[:, icx * D:(icx + 1) * D],
                                                                 in_=wstC[:, slot * 2048: slot * 2048 + D]),
                                                   inc="tV")
                        wo_ready = cast_tok[-1]

                        for icx in range(ICC):
                            cc.ins("sp", lambda e, icx=icx, hf=hf:
                                   e.dma_start(out=attn_res[:, icx * TH:(icx + 1) * TH],
                                               in_=attn_a[icx][:, hf * TH:(hf + 1) * TH]),
                                   inc="dS", k=16)
                        attn_tok = cc.tok("dS")

                        nsl = (TH + 511) // 512
                        for oc in range(DC):
                            pbk = oc % 2
                            cc.waitt("pe", ps_free[pbk])
                            cc.waitt("pe", wo_ready)
                            cc.waitt("pe", attn_tok)
                            for icx in range(ICC):
                                for s in range(nsl):
                                    w = min(512, TH - s * 512)
                                    cc.ins("pe", lambda e, pbk=pbk, icx=icx, s=s, w=w, oc=oc,
                                           first=(icx == 0), last=(icx == ICC - 1):
                                           e.matmul(ps4[:, pbk * 1024 + s * 512: pbk * 1024 + s * 512 + w],
                                                    wo_bf[:, icx * D + oc * 128: icx * D + oc * 128 + 128],
                                                    attn_res[:, icx * TH + s * 512: icx * TH + s * 512 + w],
                                                    start=first, stop=last),
                                           inc="tP" if (icx == ICC - 1 and s == nsl - 1) else None)
                            grp = cc.tok("tP")
                            cc.waitt("dve", grp)
                            cc.waitt("dve", x2_ready)
                            cc.ins("dve", lambda e, pbk=pbk, oc=oc:
                                   e.tensor_tensor(out=x2[:, oc * TH:(oc + 1) * TH],
                                                   in0=ps4[:, pbk * 1024: pbk * 1024 + TH],
                                                   in1=x2[:, oc * TH:(oc + 1) * TH], op=ALU.add),
                                   inc="tV")
                            ps_free[pbk] = cc.tok("tV")

                    c.run_phase(ph_c)
                c.full_gate()

                with ExitStack() as stD:
                    xn2 = stD.enter_context(nc.sbuf_tensor(f"xn2_{hf}", [128, DC * TH], BF16))
                    gated = stD.enter_context(nc.sbuf_tensor(f"gated{hf}", [128, SUPC * TH], BF16))
                    wg_bf = stD.enter_context(nc.sbuf_tensor(f"wg_bf{hf}", [128, 2 * DC * SUP], BF16))
                    wu_bf = stD.enter_context(nc.sbuf_tensor(f"wu_bf{hf}", [128, 2 * DC * SUP], BF16))
                    wd_bf = stD.enter_context(nc.sbuf_tensor(f"wd_bf{hf}", [128, SUPC * D], BF16))
                    WSLOT = max(D, 4 * SUP)
                    wtmp_cols = max(2 * WSLOT, 3 * TH)
                    wtmp = stD.enter_context(nc.sbuf_tensor(f"wtmp{hf}", [128, wtmp_cols], F32))
                    silu_t = stD.enter_context(nc.sbuf_tensor(f"silu{hf}", [128, max(2 * 512, TH)], F32))

                    def ph_d(cc, hf=hf):
                        nsl = (TH + 511) // 512
                        rows = wtmp
                        for dcx in range(DC):
                            cc.ins("gp", lambda e, dcx=dcx:
                                   e.tensor_copy(out=xn2[:, dcx * TH:(dcx + 1) * TH],
                                                 in_=x2[:, dcx * TH:(dcx + 1) * TH]),
                                   inc="tG" if dcx == DC - 1 else None)
                        x2b = cc.tok("tG")
                        cc.waitt("pe", x2b)
                        for s in range(nsl):
                            w = min(512, TH - s * 512)
                            for dcx in range(DC):
                                cc.ins("pe", lambda e, s=s, w=w, dcx=dcx,
                                       first=(dcx == 0), last=(dcx == DC - 1):
                                       e.matmul(ps4[0:1, s * 512: s * 512 + w],
                                                ones_neg[:],
                                                xn2[:, dcx * TH + s * 512: dcx * TH + s * 512 + w],
                                                start=first, stop=last),
                                       inc="tP" if dcx == DC - 1 else None)
                        sneg = cc.tok("tP")
                        cc.waitt("gp", sneg)
                        for dcx in range(DC):
                            cc.ins("gp", lambda e, dcx=dcx:
                                   e.tensor_tensor(out=xn2[:, dcx * TH:(dcx + 1) * TH],
                                                   in0=xn2[:, dcx * TH:(dcx + 1) * TH],
                                                   in1=xn2[:, dcx * TH:(dcx + 1) * TH], op=ALU.mult),
                                   inc="tG" if dcx == DC - 1 else None)
                        sq_done = cc.tok("tG")
                        cc.waitt("pe", sq_done)
                        for s in range(nsl):
                            w = min(512, TH - s * 512)
                            for dcx in range(DC):
                                cc.ins("pe", lambda e, s=s, w=w, dcx=dcx,
                                       first=(dcx == 0), last=(dcx == DC - 1):
                                       e.matmul(ps4[0:1, 1024 + s * 512: 1024 + s * 512 + w],
                                                ones_inv[:],
                                                xn2[:, dcx * TH + s * 512: dcx * TH + s * 512 + w],
                                                start=first, stop=last),
                                       inc="tP" if dcx == DC - 1 else None)
                        ssq = cc.tok("tP")
                        cc.waitt("dve", sneg)
                        cc.ins("dve", lambda e: e.tensor_copy(out=rows[0:1, 0:TH], in_=ps4[0:1, 0:TH]), inc="tV")
                        cc.waitt("dve", ssq)
                        cc.ins("dve", lambda e: e.tensor_copy(out=rows[0:1, TH:2 * TH], in_=ps4[0:1, 1024:1024 + TH]), inc="tV")
                        cc.wait("dve", "tV", cc.state["sems"]["tV"])
                        cc.ins("dve", lambda e:
                               e.tensor_tensor(out=rows[0:1, 2 * TH:3 * TH], in0=rows[0:1, 0:TH],
                                               in1=rows[0:1, 0:TH], op=ALU.mult), inc="tV")
                        cc.wait("dve", "tV", cc.state["sems"]["tV"])
                        cc.ins("dve", lambda e:
                               e.tensor_tensor(out=rows[0:1, TH:2 * TH], in0=rows[0:1, TH:2 * TH],
                                               in1=rows[0:1, 2 * TH:3 * TH], op=ALU.subtract), inc="tV")
                        var_tok = cc.tok("tV")
                        cc.waitt("act", var_tok)
                        cc.ins("act", lambda e:
                               e.activation(out=rows[0:1, 2 * TH:3 * TH], in_=rows[0:1, TH:2 * TH],
                                            func=AF.Ln, bias=epsb[0:1, 0:1]), inc="tA")
                        cc.wait("act", "tA", cc.state["sems"]["tA"])
                        cc.ins("act", lambda e:
                               e.activation(out=rows[0:1, 2 * TH:3 * TH], in_=rows[0:1, 2 * TH:3 * TH],
                                            func=AF.Exp, scale=-0.5), inc="tA")
                        rstd_tok = cc.tok("tA")
                        cc.waitt("pe", rstd_tok)
                        for s in range(nsl):
                            w = min(512, TH - s * 512)
                            cc.ins("pe", lambda e, s=s, w=w:
                                   e.matmul(ps4[:, s * 512: s * 512 + w], ones_row[:],
                                            rows[0:1, s * 512: s * 512 + w], start=True, stop=True),
                                   inc="tP")
                            cc.ins("pe", lambda e, s=s, w=w:
                                   e.matmul(ps4[:, 1024 + s * 512: 1024 + s * 512 + w], ones_row[:],
                                            rows[0:1, 2 * TH + s * 512: 2 * TH + s * 512 + w],
                                            start=True, stop=True),
                                   inc="tP")
                        bc_tok = cc.tok("tP")
                        cc.waitt("dve", bc_tok)
                        for dcx in range(DC):
                            cc.ins("dve", lambda e, dcx=dcx:
                                   e.tensor_tensor(out=silu_t[:, 0:TH], in0=x2[:, dcx * TH:(dcx + 1) * TH],
                                                   in1=ps4[:, 0:TH], op=ALU.add), inc="tV")
                            cc.wait("dve", "tV", cc.state["sems"]["tV"])
                            cc.ins("dve", lambda e, dcx=dcx:
                                   e.tensor_tensor(out=xn2[:, dcx * TH:(dcx + 1) * TH],
                                                   in0=silu_t[:, 0:TH], in1=ps4[:, 1024:1024 + TH],
                                                   op=ALU.mult), inc="tV")
                            cc.wait("dve", "tV", cc.state["sems"]["tV"])
                        xn2_tok = cc.tok("tV")

                        DGRP = DC // 4 if DC % 4 == 0 else None
                        st_free = [bc_tok, bc_tok]
                        pj = [0]
                        gu_tok = [None, None]
                        wd_tok_box = [None]
                        wgu_free = [None, None]
                        wd_free = [None]

                        def load_gu(s):
                            sb = s % 2
                            toks = []
                            for wsrc, wdst in ((wgT_d, wg_bf), (wuT_d, wu_bf)):
                                for g in range(DC // 4):
                                    slot = pj[0] % 2
                                    pj[0] += 1
                                    dw = "dW0" if slot == 0 else "dW1"
                                    cc.waitt("sp", st_free[slot])
                                    src = bass.AP(wsrc, (4 * g * 128) * F + s * SUP,
                                                  [[F, 128], [128 * F, 4], [1, SUP]])
                                    dst_ap = bass.AP(wtmp, slot * WSLOT,
                                                     [[wtmp_cols, 128], [SUP, 4], [1, SUP]])
                                    cc.ins("sp", lambda e, src=src, dst_ap=dst_ap:
                                           e.dma_start(out=dst_ap, in_=src), inc=dw, k=16)
                                    cc.wait("dve", dw, cc.state["sems"][dw])
                                    cc.waitt("dve", wgu_free[sb])
                                    tokc = cc.ins("dve", lambda e, slot=slot, wdst=wdst, sb=sb, g=g:
                                                  e.tensor_copy(out=wdst[:, sb * DC * SUP + 4 * g * SUP: sb * DC * SUP + 4 * (g + 1) * SUP],
                                                                in_=wtmp[:, slot * WSLOT: slot * WSLOT + 4 * SUP]),
                                                  inc="tV")
                                    st_free[slot] = tokc
                                    toks.append(tokc)
                            gu_tok[sb] = toks[-1]

                        def load_wd(s):
                            toks = []
                            for hcx in range(SUPC):
                                slot = pj[0] % 2
                                pj[0] += 1
                                dw = "dW0" if slot == 0 else "dW1"
                                cc.waitt("sp", st_free[slot])
                                src = wdT_d.ap()[s * SUP + hcx * 128: s * SUP + (hcx + 1) * 128, :]
                                cc.ins("sp", lambda e, src=src, slot=slot:
                                       e.dma_start(out=wtmp[:, slot * WSLOT: slot * WSLOT + D], in_=src),
                                       inc=dw, k=16)
                                cc.wait("dve", dw, cc.state["sems"][dw])
                                cc.waitt("dve", wd_free[0])
                                tokc = cc.ins("dve", lambda e, slot=slot, hcx=hcx:
                                              e.tensor_copy(out=wd_bf[:, hcx * D:(hcx + 1) * D],
                                                            in_=wtmp[:, slot * WSLOT: slot * WSLOT + D]),
                                              inc="tV")
                                st_free[slot] = tokc
                                toks.append(tokc)
                            wd_tok_box[0] = toks[-1]

                        silu_free = [None, None]
                        gbank = [None, None, None, None]
                        dpsf = [None, None]
                        gated_free = None

                        load_gu(0)
                        load_wd(0)
                        for sp_i in range(NSUP):
                            sb = sp_i % 2
                            wd_tok = wd_tok_box[0]
                            if sp_i + 1 < NSUP:
                                load_gu(sp_i + 1)
                            for hcx in range(SUPC):
                                for s in range(nsl):
                                    w = min(512, TH - s * 512)
                                    gb = s % 2
                                    ub = 2 + s % 2
                                    cc.waitt("pe", gbank[gb])
                                    cc.waitt("pe", gbank[ub])
                                    cc.waitt("pe", gu_tok[sb])
                                    cc.waitt("pe", xn2_tok)
                                    for dcx in range(DC):
                                        cc.ins("pe", lambda e, gb=gb, w=w, s=s, dcx=dcx, hcx=hcx, sb=sb,
                                               first=(dcx == 0), last=(dcx == DC - 1):
                                               e.matmul(ps4[:, gb * 512: gb * 512 + w],
                                                        wg_bf[:, sb * DC * SUP + dcx * SUP + hcx * 128: sb * DC * SUP + dcx * SUP + (hcx + 1) * 128],
                                                        xn2[:, dcx * TH + s * 512: dcx * TH + s * 512 + w],
                                                        start=first, stop=last),
                                               inc="tP" if dcx == DC - 1 else None)
                                    g_grp = cc.tok("tP")
                                    for dcx in range(DC):
                                        cc.ins("pe", lambda e, ub=ub, w=w, s=s, dcx=dcx, hcx=hcx, sb=sb,
                                               first=(dcx == 0), last=(dcx == DC - 1):
                                               e.matmul(ps4[:, ub * 512: ub * 512 + w],
                                                        wu_bf[:, sb * DC * SUP + dcx * SUP + hcx * 128: sb * DC * SUP + dcx * SUP + (hcx + 1) * 128],
                                                        xn2[:, dcx * TH + s * 512: dcx * TH + s * 512 + w],
                                                        start=first, stop=last),
                                               inc="tP" if dcx == DC - 1 else None)
                                    u_grp = cc.tok("tP")
                                    cc.waitt("act", g_grp)
                                    cc.waitt("act", silu_free[s % 2])
                                    cc.ins("act", lambda e, gb=gb, w=w, s=s:
                                           e.activation(out=silu_t[:, (s % 2) * 512:(s % 2) * 512 + w],
                                                        in_=ps4[:, gb * 512: gb * 512 + w],
                                                        func=AF.Silu), inc="tA")
                                    sil = cc.tok("tA")
                                    gbank[gb] = sil
                                    cc.waitt("dve", sil)
                                    cc.waitt("dve", u_grp)
                                    if sp_i > 0 or hcx > 0 or s > 0:
                                        cc.waitt("dve", gated_free)
                                    cc.ins("dve", lambda e, ub=ub, w=w, hcx=hcx, s=s:
                                           e.tensor_tensor(out=gated[:, hcx * TH + s * 512: hcx * TH + s * 512 + w],
                                                           in0=silu_t[:, (s % 2) * 512:(s % 2) * 512 + w],
                                                           in1=ps4[:, ub * 512: ub * 512 + w], op=ALU.mult),
                                           inc="tV")
                                    gbank[ub] = cc.tok("tV")
                                    silu_free[s % 2] = gbank[ub]
                            gated_tok = cc.tok("tV")
                            wgu_free[sb] = cc.tok("tP")
                            cc.waitt("pe", gated_tok)
                            cc.waitt("pe", wd_tok)
                            for dcx in range(DC):
                                for s in range(nsl):
                                    w = min(512, TH - s * 512)
                                    pbi = (dcx * nsl + s) % 2
                                    pb = (pex, pvb)[pbi]
                                    cc.waitt("pe", dpsf[pbi])
                                    for hcx in range(SUPC):
                                        cc.ins("pe", lambda e, pb=pb, w=w, dcx=dcx, s=s, hcx=hcx,
                                               first=(hcx == 0), last=(hcx == SUPC - 1):
                                               e.matmul(pb[:, 0:w],
                                                        wd_bf[:, hcx * D + dcx * 128: hcx * D + (dcx + 1) * 128],
                                                        gated[:, hcx * TH + s * 512: hcx * TH + s * 512 + w],
                                                        start=first, stop=last),
                                               inc="tP" if hcx == SUPC - 1 else None)
                                    dgrp = cc.tok("tP")
                                    cc.waitt("dve", dgrp)
                                    cc.ins("dve", lambda e, pb=pb, w=w, dcx=dcx, s=s:
                                           e.tensor_tensor(out=x2[:, dcx * TH + s * 512: dcx * TH + s * 512 + w],
                                                           in0=pb[:, 0:w],
                                                           in1=x2[:, dcx * TH + s * 512: dcx * TH + s * 512 + w],
                                                           op=ALU.add),
                                           inc="tV")
                                    dpsf[pbi] = cc.tok("tV")
                            gated_free = cc.tok("tP")
                            wd_free[0] = cc.tok("tP")
                            if sp_i + 1 < NSUP:
                                load_wd(sp_i + 1)
                        ffn_done = cc.tok("tV")

                        ost_free = [st_free[0], st_free[1]]
                        bank_free2 = [None, None]
                        for ii in range(NTH):
                            i = hf * NTH + ii
                            osl = ii % 2
                            cc.waitt("pe", ffn_done)
                            for r in range(DC // 4):
                                pb = (pex, pvb)[r % 2]
                                cc.waitt("pe", bank_free2[r % 2])
                                for j in range(4):
                                    dcx = r * 4 + j
                                    cc.ins("pe", lambda e, dcx=dcx, ii=ii, j=j, pb=pb:
                                           e.transpose(pb[:, j * 128:(j + 1) * 128],
                                                       x2[:, dcx * TH + ii * 128: dcx * TH + (ii + 1) * 128],
                                                       idn_f[:]),
                                           inc="tP" if j == 3 else None)
                                tr = cc.tok("tP")
                                cc.waitt("dve", tr)
                                cc.waitt("dve", ost_free[osl])
                                cc.ins("dve", lambda e, r=r, osl=osl, pb=pb:
                                       e.tensor_copy(out=wtmp[:, osl * WSLOT + r * 512: osl * WSLOT + (r + 1) * 512],
                                                     in_=pb[:, 0:512]),
                                       inc="tV")
                                bank_free2[r % 2] = cc.tok("tV")
                            ev = cc.tok("tV")
                            cc.waitt("sp", ev)
                            dy = "dY0" if osl == 0 else "dY1"
                            cc.ins("sp", lambda e, i=i, osl=osl:
                                   e.dma_start(out=ya[i], in_=wtmp[:, osl * WSLOT: osl * WSLOT + D]),
                                   inc=dy, k=16)
                            ost_free[osl] = cc.tok(dy)

                    c.run_phase(ph_d)
                c.full_gate()

        def ph_fin(cc):
            for e in ENGINES:
                for snm in list(cc.state["sems"].keys()):
                    if snm.startswith("d"):
                        cc.wait(e, snm, cc.state["sems"][snm])

        c.run_phase(ph_fin)

    return nc



def rope_tables(S_, dtype=np.float32):
    half = ROPE // 2
    inv = 1.0 / (10000.0 ** (np.arange(half, dtype=np.float64) * 2.0 / ROPE))
    fr = np.arange(S_, dtype=np.float64)[:, None] * inv[None, :]
    cos = np.cos(fr).astype(np.float32)
    sin = np.sin(fr).astype(np.float32)
    cosI = np.repeat(cos, 2, axis=1)
    sinI = np.empty((S_, ROPE), np.float32)
    sinI[:, 0::2] = -sin
    sinI[:, 1::2] = sin
    return cosI, sinI


def host_core_inputs(cfg, xb, weights, p):
    S, D = cfg["S"], cfg["D"]
    NT, NO = S // 128, S // 256
    gl = np.empty(NT, np.int64)
    gl[0::2] = 2 * np.arange(NO) + p
    gl[1::2] = 2 * np.arange(NO) + (1 - p)
    xp = np.ascontiguousarray(xb.reshape(NT, 128, D)[gl].reshape(S, D))

    cosI, sinI = rope_tables(S)
    ck = cosI.reshape(NT, 128, ROPE)[gl]
    sk = sinI.reshape(NT, 128, ROPE)[gl]
    SCALE = float(HD) ** -0.5
    ck_sb = np.ascontiguousarray((ck * SCALE).transpose(1, 0, 2).reshape(128, NT * ROPE))
    sk_sb = np.ascontiguousarray((sk * SCALE).transpose(1, 0, 2).reshape(128, NT * ROPE))
    own = 2 * np.arange(NO) + p
    cq = cosI.reshape(NT, 128, ROPE)[own]
    sq = sinI.reshape(NT, 128, ROPE)[own]
    cq_sb = np.ascontiguousarray(cq.transpose(1, 0, 2).reshape(128, NO * ROPE))
    sq_sb = np.ascontiguousarray(sq.transpose(1, 0, 2).reshape(128, NO * ROPE))

    masks = np.zeros((NO, 128, 512), np.float32)
    for i in range(NO):
        E = 256 * (i + 1)
        w = 512 if E % 512 == 0 else E % 512
        q_glob = 128 * (2 * i + p) + np.arange(128)
        kslot = np.arange(E - w, E)
        ktile = kslot // 128
        k_glob = 128 * gl[ktile] + (kslot % 128)
        m = np.where(k_glob[None, :] > q_glob[:, None], MASK_NEG, 0.0)
        masks[i, :, 512 - w:] = m
        masks[i, :, :512 - w] = 0.0
    masks_sb = np.ascontiguousarray(masks.transpose(1, 0, 2).reshape(128, NO * 512)).astype(ml_dtypes.bfloat16)

    d = dict(weights)
    d["x"] = xp
    d["cosq"] = cq_sb
    d["sinq"] = sq_sb
    d["cosk"] = ck_sb
    d["sink"] = sk_sb
    d["masks"] = masks_sb
    d["idnf"] = np.eye(128, dtype=np.float32)
    d["idnb"] = np.eye(128, dtype=np.float32).astype(ml_dtypes.bfloat16)
    return d, gl, own


_prog_cache = {}


def kernel(L0_attn_hidden_states, L0_attn_Wqa, L0_attn_Wqb, L0_attn_Wkva,
           L0_attn_Wkvb, L0_attn_Wo, L0_ffn_w_gate, L0_ffn_w_up, L0_ffn_w_down):
    cfg = CFG_FULL
    S, D = cfg["S"], cfg["D"]
    B = L0_attn_hidden_states.shape[0]
    NO = S // 256

    key = "full"
    if key not in _prog_cache:
        _prog_cache[key] = build_program(cfg)
    nc = _prog_cache[key]

    weights = {
        "wqa": np.ascontiguousarray(L0_attn_Wqa, np.float32),
        "wqb": np.ascontiguousarray(L0_attn_Wqb, np.float32),
        "wkva": np.ascontiguousarray(L0_attn_Wkva, np.float32),
        "wkvb": np.ascontiguousarray(L0_attn_Wkvb, np.float32),
        "wo": np.ascontiguousarray(L0_attn_Wo, np.float32),
        "wgT": np.ascontiguousarray(np.asarray(L0_ffn_w_gate, np.float32).T),
        "wuT": np.ascontiguousarray(np.asarray(L0_ffn_w_up, np.float32).T),
        "wdT": np.ascontiguousarray(np.asarray(L0_ffn_w_down, np.float32).T),
    }
    x_full = np.asarray(L0_attn_hidden_states, np.float32)

    in_maps = []
    owns = []
    for core in range(8):
        b, p = core // 2, core % 2
        d, gl, own = host_core_inputs(cfg, x_full[b], weights, p)
        in_maps.append(d)
        owns.append(own)

    res = run_bass_kernel_spmd(nc, in_maps, core_ids=list(range(8)))

    out = np.empty((B, S, D), np.float32)
    for core in range(8):
        b, p = core // 2, core % 2
        y = res.results[core]["y"].reshape(NO, 128, D)
        out[b].reshape(S // 128, 128, D)[owns[core]] = y
    return out


# version 08
# speedup vs baseline: 1.0897x; 1.0897x over previous
import numpy as np
import ml_dtypes
import copy
from contextlib import ExitStack

import concourse.bass as bass
import concourse.mybir as mybir
from concourse.bass_utils import run_bass_kernel_spmd

F32 = mybir.dt.float32
BF16 = mybir.dt.bfloat16
AF = mybir.ActivationFunctionType
ALU = mybir.AluOpType
AX = mybir.AxisListType

MASK_NEG = -30000.0
EPS = 1e-5

CFG_FULL = dict(S=4096, D=2048, QL=1536, KVL=512, H=16, F=8192)
CFG_SMALL = dict(S=1024, D=512, QL=256, KVL=256, H=2, F=1024)

ROPE = 64
NOPE = 64
VD = 128
HD = NOPE + ROPE
HKV = NOPE + VD
SEG = 2048

ENGINES = ("pe", "dve", "act", "gp", "sp")


def seg_list(E):
    segs = []
    rem = E
    while rem > SEG:
        segs.append(SEG)
        rem -= SEG
    segs.append(rem)
    return segs


def slab_list(width):
    out = []
    k = 0
    while k < width:
        w = min(512, width - k)
        out.append((k, w))
        k += w
    return out


def swap_view(ap_2d, n):
    return bass.AP(ap_2d.tensor, ap_2d.offset + 1,
                   [list(ap_2d.ap[0]), [2, n // 2], [-1, 2]])


class Ctx:

    def __init__(self, block, handles):
        self.block = block
        self.handles = handles
        self.state = {"sems": {n: 0 for n in handles},
                      "wm": {e: {} for e in ENGINES}}
        self.cur = None
        self.eng = None

    def snap(self):
        return copy.deepcopy(self.state)

    def restore(self, s):
        self.state = copy.deepcopy(s)

    def wait(self, engine, sem, val):
        if val <= 0:
            return
        wm = self.state["wm"][engine]
        if wm.get(sem, 0) >= val:
            return
        wm[sem] = val
        if self.cur == engine:
            self.eng.wait_ge(self.handles[sem], val)

    def waitt(self, engine, tok):
        if tok is not None:
            self.wait(engine, tok[0], tok[1])

    def ins(self, engine, fn, inc=None, k=1):
        if inc is not None:
            self.state["sems"][inc] += k
        if self.cur == engine:
            bi = fn(self.eng)
            if inc is not None:
                bi.then_inc(self.handles[inc], k)
        return (inc, self.state["sems"][inc]) if inc is not None else None

    def tok(self, sem):
        return (sem, self.state["sems"][sem])

    def run_phase(self, phase_fn):
        s0 = self.snap()
        calls = [("pe", self.block.tensor), ("dve", self.block.vector),
                 ("act", self.block.scalar), ("gp", self.block.gpsimd),
                 ("sp", self.block.sync)]
        for name, bc in calls:
            self.restore(s0)

            def body(eng, name=name):
                self.cur = name
                self.eng = eng
                phase_fn(self)
                self.cur = None
                self.eng = None

            bc(body)

    def full_gate(self):
        def g(c):
            for e in ENGINES:
                for sem, v in list(c.state["sems"].items()):
                    c.wait(e, sem, v)
        self.run_phase(g)


def build_program(cfg):
    S, D, QL, KVL, H, F = cfg["S"], cfg["D"], cfg["QL"], cfg["KVL"], cfg["H"], cfg["F"]
    KVOUT = KVL + ROPE
    DC, QLC, KVC, FC = D // 128, QL // 128, KVL // 128, F // 128
    NT, NO = S // 128, S // 256
    TOK = NO * 128
    TH = TOK // 2
    NTH = TH // 128
    SCALE = float(HD) ** -0.5
    SUP = 512
    NSUP = F // SUP
    SUPC = SUP // 128

    nc = bass.Bass("TRN2", target_bir_lowering=False)

    x_d = nc.dram_tensor("x", [S, D], F32, kind="ExternalInput")
    wqa_d = nc.dram_tensor("wqa", [D, QL], F32, kind="ExternalInput")
    wqb_d = nc.dram_tensor("wqb", [QL, H * HD], F32, kind="ExternalInput")
    wkva_d = nc.dram_tensor("wkva", [D, KVOUT], F32, kind="ExternalInput")
    wkvb_d = nc.dram_tensor("wkvb", [KVL, H * HKV], F32, kind="ExternalInput")
    wo_d = nc.dram_tensor("wo", [H * VD, D], F32, kind="ExternalInput")
    wgT_d = nc.dram_tensor("wgT", [D, F], F32, kind="ExternalInput")
    wuT_d = nc.dram_tensor("wuT", [D, F], F32, kind="ExternalInput")
    wdT_d = nc.dram_tensor("wdT", [F, D], F32, kind="ExternalInput")
    idnf_d = nc.dram_tensor("idnf", [128, 128], F32, kind="ExternalInput")
    idnb_d = nc.dram_tensor("idnb", [128, 128], BF16, kind="ExternalInput")
    cosq_d = nc.dram_tensor("cosq", [128, NO * ROPE], F32, kind="ExternalInput")
    sinq_d = nc.dram_tensor("sinq", [128, NO * ROPE], F32, kind="ExternalInput")
    cosk_d = nc.dram_tensor("cosk", [128, NT * ROPE], F32, kind="ExternalInput")
    sink_d = nc.dram_tensor("sink", [128, NT * ROPE], F32, kind="ExternalInput")
    masks_d = nc.dram_tensor("masks", [128, NO * 512], BF16, kind="ExternalInput")
    y_d = nc.dram_tensor("y", [TOK, D], F32, kind="ExternalOutput")
    qlat_sc = nc.dram_tensor("qlat_sc", [NO, 128, QL], BF16, kind="Internal")
    q_sc = nc.dram_tensor("q_sc", [NO, 128, H * HD], BF16, kind="Internal")
    attn_sc = nc.dram_tensor("attn_sc", [H, 128, TOK], BF16, kind="Internal")

    xa = x_d.ap().rearrange("(t r) d -> t r d", r=128)
    ya = y_d.ap().rearrange("(t r) d -> t r d", r=128)
    qlat_a = qlat_sc.ap()
    q_a = q_sc.ap()
    attn_a = attn_sc.ap()

    with ExitStack() as st:
        idn_f = st.enter_context(nc.sbuf_tensor("idn_f", [128, 128], F32))
        idn_b = st.enter_context(nc.sbuf_tensor("idn_b", [128, 128], BF16))
        ones_neg = st.enter_context(nc.sbuf_tensor("ones_neg", [128, 1], BF16))
        ones_inv = st.enter_context(nc.sbuf_tensor("ones_inv", [128, 1], BF16))
        ones_row = st.enter_context(nc.sbuf_tensor("ones_row", [1, 128], F32))
        sm = st.enter_context(nc.sbuf_tensor("sm", [128, 32], F32))
        bnb = st.enter_context(nc.sbuf_tensor("bnb", [128, 24], F32))
        oacc = st.enter_context(nc.sbuf_tensor("oacc", [128, 128], F32))
        epsb = st.enter_context(nc.sbuf_tensor("epsb", [128, 1], F32))

        ps4 = st.enter_context(nc.psum_tensor("ps4", [128, 2048], F32))
        pex = st.enter_context(nc.psum_tensor("pex", [128, 512], F32))
        pvb = st.enter_context(nc.psum_tensor("pvb", [128, 512], F32))
        ptrA = st.enter_context(nc.psum_tensor("ptrA", [128, 1024], BF16))
        ptrB = st.enter_context(nc.psum_tensor("ptrB", [128, 1024], BF16))

        semnames = ["dS", "dG", "dW0", "dW1", "dX0", "dX1", "dQ0", "dQ1", "dL0", "dL1", "dY0", "dY1", "dA0", "dA1", "tP", "tV", "tA", "tG"]
        handles = {n: st.enter_context(nc.semaphore(n)) for n in semnames}
        block = st.enter_context(nc.Block())
        c = Ctx(block, handles)


        def ph_pro(cc):
            cc.ins("sp", lambda e: e.dma_start(out=idn_f[:], in_=idnf_d.ap()), inc="dS", k=16)
            cc.ins("sp", lambda e: e.dma_start(out=idn_b[:], in_=idnb_d.ap()), inc="dS", k=16)
            cc.ins("dve", lambda e: e.memset(ones_neg[:], -1.0 / D), inc="tV")
            cc.ins("dve", lambda e: e.memset(ones_inv[:], 1.0 / D), inc="tV")
            cc.ins("dve", lambda e: e.memset(ones_row[:], 1.0), inc="tV")
            cc.ins("dve", lambda e: e.memset(epsb[:], EPS), inc="tV")

        c.run_phase(ph_pro)
        c.full_gate()

        with ExitStack() as stAB:
            c_kv = stAB.enter_context(nc.sbuf_tensor("c_kv", [128, KVC * S], BF16))
            kpe_bf = stAB.enter_context(nc.sbuf_tensor("kpe_bf", [64, S], BF16))
            wkvb_bf = stAB.enter_context(nc.sbuf_tensor("wkvb_bf", [128, KVC * H * HKV], BF16))

            with ExitStack() as stA:
                cosk = stA.enter_context(nc.sbuf_tensor("cosk_s", [128, NT * ROPE], F32))
                sink = stA.enter_context(nc.sbuf_tensor("sink_s", [128, NT * ROPE], F32))
                xring = stA.enter_context(nc.sbuf_tensor("xring", [128, 2 * D], F32))
                wqa_bf = stA.enter_context(nc.sbuf_tensor("wqa_bf", [128, DC * QL], BF16))
                wkva_bf = stA.enter_context(nc.sbuf_tensor("wkva_bf", [128, DC * KVOUT], BF16))
                wst = stA.enter_context(nc.sbuf_tensor("wst", [128, 2 * 2048], F32))
                xn1tm = stA.enter_context(nc.sbuf_tensor("xn1tm", [128, 2 * D], BF16))
                xn1fm = stA.enter_context(nc.sbuf_tensor("xn1fm", [128, 2 * D], BF16))
                kst = stA.enter_context(nc.sbuf_tensor("kst", [128, 192], F32))
                qlst = stA.enter_context(nc.sbuf_tensor("qlst", [128, 2 * QL], BF16))

                def ph_a1(cc):
                    pieces = []
                    for dcx in range(DC):
                        pieces.append((wkva_d.ap(), dcx, 0, KVOUT, wkva_bf, dcx * KVOUT))
                    for dcx in range(DC):
                        off = 0
                        while off < QL:
                            w = min(2048, QL - off)
                            pieces.append((wqa_d.ap(), dcx, off, w, wqa_bf, dcx * QL + off))
                            off += w
                    for ccx in range(KVC):
                        off = 0
                        while off < H * HKV:
                            w = min(2048, H * HKV - off)
                            pieces.append((wkvb_d.ap(), ccx, off, w, wkvb_bf, ccx * H * HKV + off))
                            off += w
                    npc = len(pieces)
                    cast_tok = [None] * npc
                    kva_last = [None]
                    qa_last = [None]

                    for j in range(npc):
                        src, chk, off, w, dst, doff = pieces[j]
                        slot = j % 2
                        if j >= 2:
                            cc.waitt("sp", cast_tok[j - 2])
                        dw = "dW0" if slot == 0 else "dW1"
                        cc.ins("sp", lambda e, src=src, chk=chk, off=off, w=w, slot=slot:
                               e.dma_start(out=wst[:, slot * 2048: slot * 2048 + w],
                                           in_=src[chk * 128:(chk + 1) * 128, off:off + w]),
                               inc=dw, k=16)
                        cc.wait("gp", dw, cc.state["sems"][dw])
                        tokc = cc.ins("gp", lambda e, slot=slot, w=w, dst=dst, doff=doff:
                                      e.tensor_copy(out=dst[:, doff:doff + w],
                                                    in_=wst[:, slot * 2048: slot * 2048 + w]),
                                      inc="tG")
                        cast_tok[j] = tokc
                        if j == DC - 1:
                            kva_last[0] = tokc
                        if j == npc - KVC * ((H * HKV + 2047) // 2048) - 1:
                            qa_last[0] = tokc

                    for buf, src in ((cosk, cosk_d), (sink, sink_d)):
                        cc.ins("sp", lambda e, buf=buf, src=src:
                               e.dma_start(out=buf[:], in_=src.ap()), inc="dS", k=16)
                    tabs_tok = cc.tok("dS")

                    xr_free = [None, None]
                    tm_free = [None, None]
                    fm_free = [None, None]
                    ql_free = [None, None]
                    ptr_free = [None, None]
                    bank_free = [None, None]
                    pex_free = [None]

                    for t in range(NT):
                        sl = t % 2
                        dx = "dX0" if sl == 0 else "dX1"
                        cc.waitt("gp", xr_free[sl])
                        cc.ins("gp", lambda e, t=t, sl=sl:
                               e.dma_start(out=xring[:, sl * D:(sl + 1) * D], in_=xa[t]),
                               inc=dx, k=16)
                        cc.wait("dve", dx, cc.state["sems"][dx])
                        nch = (D + 511) // 512
                        for j in range(nch):
                            w = min(512, D - j * 512)
                            cc.ins("dve", lambda e, sl=sl, j=j, w=w:
                                   e.bn_stats(bnb[:, j * 6:(j + 1) * 6],
                                              xring[:, sl * D + j * 512: sl * D + j * 512 + w]),
                                   inc="tV" if j == nch - 1 else None)
                        cc.wait("dve", "tV", cc.state["sems"]["tV"])
                        cc.ins("dve", lambda e, nch=nch: e.bn_aggr(sm[:, 0:2], bnb[:, 0:6 * nch]), inc="tV")
                        agg = cc.tok("tV")
                        cc.waitt("act", agg)
                        cc.ins("act", lambda e: e.activation(out=sm[:, 2:3], in_=sm[:, 1:2],
                                                             func=AF.Ln, bias=epsb[:, 0:1]), inc="tA")
                        cc.wait("act", "tA", cc.state["sems"]["tA"])
                        cc.ins("act", lambda e: e.activation(out=sm[:, 2:3], in_=sm[:, 2:3],
                                                             func=AF.Exp, scale=-0.5), inc="tA")
                        rstd = cc.tok("tA")
                        cc.waitt("dve", rstd)
                        cc.waitt("dve", tm_free[sl])
                        cc.ins("dve", lambda e, sl=sl:
                               e.tensor_scalar(out=xn1tm[:, sl * D:(sl + 1) * D],
                                               in0=xring[:, sl * D:(sl + 1) * D],
                                               scalar1=sm[:, 0:1], scalar2=sm[:, 2:3],
                                               op0=ALU.subtract, op1=ALU.mult),
                               inc="tV")
                        norm = cc.tok("tV")
                        xr_free[sl] = norm
                        cc.waitt("pe", norm)
                        cc.waitt("pe", fm_free[sl])
                        for r in range(DC // 4):
                            half = r % 2
                            ptr_t = (ptrA, ptrB)[half]
                            cc.waitt("pe", ptr_free[half])
                            for j in range(4):
                                dcx = r * 4 + j
                                cc.ins("pe", lambda e, sl=sl, dcx=dcx, j=j, ptr_t=ptr_t:
                                       e.transpose(ptr_t[:, j * 128:(j + 1) * 128],
                                                   xn1tm[:, sl * D + dcx * 128: sl * D + (dcx + 1) * 128],
                                                   idn_b[:]),
                                       inc="tP" if j == 3 else None)
                            tr = cc.tok("tP")
                            cc.waitt("dve", tr)
                            cc.ins("dve", lambda e, sl=sl, r=r, ptr_t=ptr_t:
                                   e.tensor_copy(out=xn1fm[:, sl * D + r * 512: sl * D + (r + 1) * 512],
                                                 in_=ptr_t[:, 0:512]),
                                   inc="tV")
                            ptr_free[half] = cc.tok("tV")
                        tm_free[sl] = cc.tok("tP")
                        fm_ready = cc.tok("tV")

                        cc.waitt("pe", fm_ready)
                        cc.waitt("pe", kva_last[0])
                        oc_list = [("kva", o) for o in range(KVC + 1)]
                        if t % 2 == 0:
                            oc_list += [("qa", o) for o in range(QLC)]
                        for idx, (kind, oc) in enumerate(oc_list):
                            if kind == "qa":
                                cc.waitt("pe", qa_last[0])
                            bk = idx % 2
                            bko = bk * 512
                            cc.waitt("pe", bank_free[bk])
                            is_kpe = (kind == "kva" and oc == KVC)
                            m = 64 if is_kpe else 128
                            for dcx in range(DC):
                                if kind == "kva":
                                    lo = dcx * KVOUT + oc * 128
                                    lhs = wkva_bf
                                else:
                                    lo = dcx * QL + oc * 128
                                    lhs = wqa_bf
                                cc.ins("pe", lambda e, bko=bko, lhs=lhs, lo=lo, m=m, sl=sl, dcx=dcx,
                                       first=(dcx == 0), last=(dcx == DC - 1):
                                       e.matmul(ps4[0:m, bko:bko + 128],
                                                lhs[:, lo:lo + m],
                                                xn1fm[:, sl * D + dcx * 128: sl * D + (dcx + 1) * 128],
                                                start=first, stop=last),
                                       inc="tP" if dcx == DC - 1 else None)
                            grp = cc.tok("tP")
                            if kind == "kva" and oc < KVC:
                                cc.waitt("act", grp)
                                cc.ins("act", lambda e, bko=bko, oc=oc, t=t:
                                       e.activation(out=c_kv[:, oc * S + t * 128: oc * S + (t + 1) * 128],
                                                    in_=ps4[:, bko:bko + 128], func=AF.Copy),
                                       inc="tA")
                                bank_free[bk] = cc.tok("tA")
                            elif is_kpe:
                                cc.waitt("act", grp)
                                cc.ins("act", lambda e, bko=bko:
                                       e.activation(out=kst[0:64, 0:128],
                                                    in_=ps4[0:64, bko:bko + 128], func=AF.Copy),
                                       inc="tA")
                                kc = cc.tok("tA")
                                bank_free[bk] = kc
                                cc.waitt("pe", kc)
                                cc.waitt("pe", pex_free[0])
                                cc.ins("pe", lambda e: e.transpose(pex[:, 0:64], kst[0:64, 0:128], idn_f[0:64, 0:64]),
                                       inc="tP")
                                ktr = cc.tok("tP")
                                cc.waitt("dve", ktr)
                                cc.waitt("dve", tabs_tok)
                                cc.ins("dve", lambda e, t=t:
                                       e.tensor_tensor(out=kst[:, 64:128], in0=pex[:, 0:64],
                                                       in1=cosk[:, t * ROPE:(t + 1) * ROPE], op=ALU.mult),
                                       inc="tV")
                                cc.ins("dve", lambda e, t=t:
                                       e.tensor_tensor(out=kst[:, 128:192],
                                                       in0=swap_view(pex[:, 0:64], 64),
                                                       in1=sink[:, t * ROPE:(t + 1) * ROPE], op=ALU.mult),
                                       inc="tV")
                                cc.wait("dve", "tV", cc.state["sems"]["tV"])
                                cc.ins("dve", lambda e:
                                       e.tensor_tensor(out=kst[:, 64:128], in0=kst[:, 64:128],
                                                       in1=kst[:, 128:192], op=ALU.add),
                                       inc="tV")
                                krp = cc.tok("tV")
                                cc.waitt("pe", krp)
                                cc.ins("pe", lambda e: e.transpose(pex[0:64, 128:256], kst[:, 64:128], idn_f[:]),
                                       inc="tP")
                                ktr2 = cc.tok("tP")
                                cc.waitt("act", ktr2)
                                cc.ins("act", lambda e, t=t:
                                       e.activation(out=kpe_bf[0:64, t * 128:(t + 1) * 128],
                                                    in_=pex[0:64, 128:256], func=AF.Copy),
                                       inc="tA")
                                pex_free[0] = cc.tok("tA")
                            else:
                                qsl = (t // 2) % 2
                                cc.waitt("dve", grp)
                                if oc == 0:
                                    cc.waitt("dve", ql_free[qsl])
                                cc.ins("dve", lambda e, bko=bko, oc=oc, qsl=qsl:
                                       e.tensor_copy(out=qlst[:, qsl * QL + oc * 128: qsl * QL + (oc + 1) * 128],
                                                     in_=ps4[:, bko:bko + 128]),
                                       inc="tV")
                                bank_free[bk] = cc.tok("tV")
                                if oc == QLC - 1:
                                    qev = cc.tok("tV")
                                    cc.waitt("sp", qev)
                                    dl = "dL0" if qsl == 0 else "dL1"
                                    cc.ins("sp", lambda e, t=t, qsl=qsl:
                                           e.dma_start(out=qlat_a[t // 2],
                                                       in_=qlst[:, qsl * QL:(qsl + 1) * QL]),
                                           inc=dl, k=16)
                                    ql_free[qsl] = cc.tok(dl)
                        fm_free[sl] = cc.tok("tP")

                c.run_phase(ph_a1)
            c.full_gate()

            with ExitStack() as stA2:
                cosq = stA2.enter_context(nc.sbuf_tensor("cosq2", [128, NO * ROPE], F32))
                sinq = stA2.enter_context(nc.sbuf_tensor("sinq2", [128, NO * ROPE], F32))
                wqb_bf = stA2.enter_context(nc.sbuf_tensor("wqb_bf", [128, QLC * H * HD], BF16))
                wst2 = stA2.enter_context(nc.sbuf_tensor("wst2", [128, 2 * 2048], F32))
                qlring = stA2.enter_context(nc.sbuf_tensor("qlring", [128, 2 * QL], BF16))
                qstage = stA2.enter_context(nc.sbuf_tensor("qstage", [128, 2 * H * HD], BF16))

                def ph_a2(cc):
                    for buf, src in ((cosq, cosq_d), (sinq, sinq_d)):
                        cc.ins("sp", lambda e, buf=buf, src=src:
                               e.dma_start(out=buf[:], in_=src.ap()), inc="dS", k=16)
                    tabs = cc.tok("dS")
                    pieces = []
                    for lcx in range(QLC):
                        off = 0
                        while off < H * HD:
                            w = min(2048, H * HD - off)
                            pieces.append((lcx, off, w))
                            off += w
                    cast_tok = [None] * len(pieces)
                    wqb_last = [None]
                    pi = [0]

                    def emit_pieces(n):
                        j = pi[0]
                        for _ in range(n):
                            if j >= len(pieces):
                                break
                            lcx, off, w = pieces[j]
                            slot = j % 2
                            if j >= 2:
                                cc.waitt("sp", cast_tok[j - 2])
                            dw = "dW0" if slot == 0 else "dW1"
                            cc.ins("sp", lambda e, lcx=lcx, off=off, w=w, slot=slot:
                                   e.dma_start(out=wst2[:, slot * 2048: slot * 2048 + w],
                                               in_=wqb_d.ap()[lcx * 128:(lcx + 1) * 128, off:off + w]),
                                   inc=dw, k=16)
                            cc.wait("gp", dw, cc.state["sems"][dw])
                            tokc = cc.ins("gp", lambda e, slot=slot, w=w, lcx=lcx, off=off:
                                          e.tensor_copy(out=wqb_bf[:, lcx * H * HD + off: lcx * H * HD + off + w],
                                                        in_=wst2[:, slot * 2048: slot * 2048 + w]),
                                          inc="tG")
                            cast_tok[j] = tokc
                            if j == len(pieces) - 1:
                                wqb_last[0] = tokc
                            j += 1
                        pi[0] = j

                    emit_pieces(len(pieces))

                    qr_free = [None, None]
                    qs_free = [None, None]
                    nslab = (H * HD + 511) // 512
                    bank_free = [None] * 4

                    for i in range(NO):
                        sl = i % 2
                        dq = "dQ0" if sl == 0 else "dQ1"
                        cc.waitt("gp", qr_free[sl])
                        cc.ins("gp", lambda e, i=i, sl=sl:
                               e.dma_start(out=qlring[:, sl * QL:(sl + 1) * QL], in_=qlat_a[i]),
                               inc=dq, k=16)
                        qload = cc.tok(dq)
                        cc.waitt("pe", qload)
                        cc.waitt("pe", wqb_last[0])
                        if i == 0:
                            pass
                        for s in range(nslab):
                            w = min(512, H * HD - s * 512)
                            bk = s % 4
                            cc.waitt("pe", bank_free[bk])
                            for lcx in range(QLC):
                                cc.ins("pe", lambda e, bk=bk, w=w, lcx=lcx, s=s, sl=sl,
                                       first=(lcx == 0), last=(lcx == QLC - 1):
                                       e.matmul(ps4[:, bk * 512: bk * 512 + w],
                                                qlring[:, sl * QL + lcx * 128: sl * QL + (lcx + 1) * 128],
                                                wqb_bf[:, lcx * H * HD + s * 512: lcx * H * HD + s * 512 + w],
                                                start=first, stop=last),
                                       inc="tP" if lcx == QLC - 1 else None)
                            grp = cc.tok("tP")
                            cc.waitt("act", grp)
                            if s == 0:
                                cc.waitt("act", qs_free[sl])
                            cc.ins("act", lambda e, bk=bk, w=w, s=s, sl=sl:
                                   e.activation(out=qstage[:, sl * H * HD + s * 512: sl * H * HD + s * 512 + w],
                                                in_=ps4[:, bk * 512: bk * 512 + w], func=AF.Copy),
                                   inc="tA")
                            ev = cc.tok("tA")
                            cc.waitt("dve", grp)
                            cc.waitt("dve", ev)
                            cc.wait("dve", "dS", tabs[1])
                            for hh in range(w // HD):
                                pecol = bk * 512 + hh * HD + NOPE
                                qcol = sl * H * HD + s * 512 + hh * HD + NOPE
                                cc.wait("dve", "tV", cc.state["sems"]["tV"])
                                cc.ins("dve", lambda e, pecol=pecol, i=i:
                                       e.tensor_tensor(out=kst_a2[:, 0:64],
                                                       in0=ps4[:, pecol:pecol + 64],
                                                       in1=cosq[:, i * ROPE:(i + 1) * ROPE], op=ALU.mult),
                                       inc="tV")
                                cc.ins("dve", lambda e, pecol=pecol, i=i:
                                       e.tensor_tensor(out=kst_a2[:, 64:128],
                                                       in0=swap_view(ps4[:, pecol:pecol + 64], 64),
                                                       in1=sinq[:, i * ROPE:(i + 1) * ROPE], op=ALU.mult),
                                       inc="tV")
                                cc.wait("dve", "tV", cc.state["sems"]["tV"])
                                cc.ins("dve", lambda e, qcol=qcol:
                                       e.tensor_tensor(out=qstage[:, qcol:qcol + 64],
                                                       in0=kst_a2[:, 0:64], in1=kst_a2[:, 64:128],
                                                       op=ALU.add),
                                       inc="tV")
                            bank_free[bk] = cc.tok("tV")
                        qr_free[sl] = cc.tok("tP")
                        done = cc.tok("tV")
                        cc.waitt("sp", done)
                        cc.wait("sp", "tA", cc.state["sems"]["tA"])
                        dl = "dL0" if sl == 0 else "dL1"
                        cc.ins("sp", lambda e, i=i, sl=sl:
                               e.dma_start(out=q_a[i], in_=qstage[:, sl * H * HD:(sl + 1) * H * HD]),
                               inc=dl, k=16)
                        qs_free[sl] = cc.tok(dl)

                kst_a2 = stA2.enter_context(nc.sbuf_tensor("kst_a2", [128, 128], F32))
                c.run_phase(ph_a2)
            c.full_gate()

            with ExitStack() as stB:
                masks_sb = stB.enter_context(nc.sbuf_tensor("masks_sb", [128, NO * 512], BF16))
                kfull = stB.enter_context(nc.sbuf_tensor("kfull", [128, S], BF16))
                v_tm = stB.enter_context(nc.sbuf_tensor("v_tm", [128, S], BF16))
                vslab = stB.enter_context(nc.sbuf_tensor("vslab", [128, 2 * 512], BF16))
                qh = stB.enter_context(nc.sbuf_tensor("qh", [128, 2 * NO * 128], BF16))
                qTb = stB.enter_context(nc.sbuf_tensor("qTb", [128, 2 * 128], BF16))
                Pbuf = stB.enter_context(nc.sbuf_tensor("Pbuf", [128, 2 * 2048], BF16))
                PT = stB.enter_context(nc.sbuf_tensor("PT", [128, 8 * 512], BF16))
                obf = stB.enter_context(nc.sbuf_tensor("obf", [128, 2 * 128], BF16))
                astage = stB.enter_context(nc.sbuf_tensor("astage", [128, 2 * TOK], BF16))

                def ph_b(cc):
                    cc.ins("sp", lambda e: e.dma_start(out=masks_sb[:], in_=masks_d.ap()), inc="dS", k=16)
                    masks_tok = cc.tok("dS")
                    cc.ins("dve", lambda e: e.tensor_copy(out=kfull[64:128, :], in_=kpe_bf[:]), inc="tV")

                    qh_free = [None, None]
                    qh_toks = [None, None]
                    ast_free = [None, None]
                    kfull_free = None
                    vtm_free = None
                    pex_free = None
                    pv_free = None
                    acc_free = [None, None]
                    ptr_free = [None, None]
                    vs_free = [None, None]
                    ps_free = [None] * 4
                    qT_free = [None, None]
                    P_free = [None] * 2
                    PT_free = [None] * 8
                    l_free = {}
                    obf_free = [None, None]
                    ladd_prev = [None]
                    negm_free = {}
                    o_war = [None]
                    pending_o = []

                    NKS = S // 512
                    ACCB = (pvb, pex)

                    def qh_load(h):
                        hsl = h % 2
                        dq = "dQ0" if hsl == 0 else "dQ1"
                        cc.waitt("gp", qh_free[hsl])
                        qsrc = bass.AP(q_sc, h * HD,
                                       [[H * HD, 128], [128 * H * HD, NO], [1, 128]])
                        cc.ins("gp", lambda e, hsl=hsl, qsrc=qsrc:
                               e.dma_start(out=bass.AP(qh, hsl * NO * 128,
                                                       [[2 * NO * 128, 128], [128, NO], [1, 128]]),
                                           in_=qsrc),
                               inc=dq, k=16)
                        qh_toks[hsl] = cc.tok(dq)

                    for h in range(H):
                        hsl = h % 2
                        if h == 0:
                            qh_load(0)

                        cc.waitt("pe", kfull_free)
                        cc.waitt("pe", vtm_free)
                        for ks in range(NKS):
                            cc.waitt("pe", pex_free)
                            cc.waitt("pe", acc_free[1])
                            for ccx in range(KVC):
                                lo = ccx * H * HKV + h * HKV
                                cc.ins("pe", lambda e, ccx=ccx, lo=lo, ks=ks,
                                       first=(ccx == 0):
                                       e.matmul(pex[0:64, 0:512],
                                                wkvb_bf[:, lo:lo + 64],
                                                c_kv[:, ccx * S + ks * 512: ccx * S + (ks + 1) * 512],
                                                start=first, stop=(ccx == KVC - 1)),
                                       inc="tP" if ccx == KVC - 1 else None)
                            kgrp = cc.tok("tP")
                            cc.waitt("act", kgrp)
                            cc.ins("act", lambda e, ks=ks:
                                   e.activation(out=kfull[0:64, ks * 512:(ks + 1) * 512],
                                                in_=pex[0:64, 0:512], func=AF.Copy, scale=SCALE),
                                   inc="tA")
                            pex_free = cc.tok("tA")
                            acc_free[1] = pex_free
                            cc.waitt("pe", pv_free)
                            cc.waitt("pe", acc_free[0])
                            for ccx in range(KVC):
                                lo = ccx * H * HKV + h * HKV + NOPE
                                cc.ins("pe", lambda e, ccx=ccx, lo=lo, ks=ks,
                                       first=(ccx == 0):
                                       e.matmul(pvb[:, 0:512],
                                                wkvb_bf[:, lo:lo + VD],
                                                c_kv[:, ccx * S + ks * 512: ccx * S + (ks + 1) * 512],
                                                start=first, stop=(ccx == KVC - 1)),
                                       inc="tP" if ccx == KVC - 1 else None)
                            vgrp = cc.tok("tP")
                            vsl = ks % 2
                            cc.waitt("dve", vgrp)
                            cc.waitt("dve", vs_free[vsl])
                            cc.ins("dve", lambda e, vsl=vsl:
                                   e.tensor_copy(out=vslab[:, vsl * 512:(vsl + 1) * 512], in_=pvb[:, 0:512]),
                                   inc="tV")
                            vev = cc.tok("tV")
                            pv_free = vev
                            acc_free[0] = vev
                            half = ks % 2
                            ptr_t = (ptrA, ptrB)[half]
                            cc.waitt("pe", vev)
                            cc.waitt("pe", ptr_free[half])
                            for j in range(4):
                                cc.ins("pe", lambda e, vsl=vsl, j=j, ptr_t=ptr_t:
                                       e.transpose(ptr_t[:, j * 128:(j + 1) * 128],
                                                   vslab[:, vsl * 512 + j * 128: vsl * 512 + (j + 1) * 128],
                                                   idn_b[:]),
                                       inc="tP" if j == 3 else None)
                            vtr = cc.tok("tP")
                            vs_free[vsl] = vtr
                            cc.waitt("dve", vtr)
                            cc.ins("dve", lambda e, ks=ks, ptr_t=ptr_t:
                                   e.tensor_copy(out=v_tm[:, ks * 512:(ks + 1) * 512], in_=ptr_t[:, 0:512]),
                                   inc="tV")
                            ptr_free[half] = cc.tok("tV")
                        kv_ready = cc.tok("tV")
                        cc.wait("pe", "tA", cc.state["sems"]["tA"])

                        if h + 1 < H:
                            qh_load(h + 1)
                        qh_tok = qh_toks[hsl]

                        jobs = []
                        for i in range(NO):
                            E = 256 * (i + 1)
                            segs = seg_list(E)
                            sk0 = 0
                            for si, sw in enumerate(segs):
                                jobs.append(dict(i=i, si=si, sk0=sk0, sw=sw,
                                                 nseg=len(segs)))
                                sk0 += sw

                        def stage_S(jidx):
                            jb = jobs[jidx]
                            i, si, sk0, sw = jb["i"], jb["si"], jb["sk0"], jb["sw"]
                            jp = jidx % 2
                            par = i % 2
                            slabs = slab_list(sw)
                            if si == 0:
                                halfq = par
                                ptr_t = (ptrA, ptrB)[halfq]
                                cc.waitt("pe", qh_tok)
                                cc.waitt("pe", ptr_free[halfq])
                                cc.ins("pe", lambda e, hsl=hsl, i=i, ptr_t=ptr_t:
                                       e.transpose(ptr_t[:, 0:128],
                                                   qh[:, hsl * NO * 128 + i * 128: hsl * NO * 128 + (i + 1) * 128],
                                                   idn_b[:]),
                                       inc="tP")
                                qtr = cc.tok("tP")
                                cc.waitt("dve", qtr)
                                cc.waitt("dve", qT_free[par])
                                cc.ins("dve", lambda e, ptr_t=ptr_t, par=par:
                                       e.tensor_copy(out=qTb[:, par * 128:(par + 1) * 128],
                                                     in_=ptr_t[:, 0:128]), inc="tV")
                                jb["qT_tok"] = cc.tok("tV")
                                ptr_free[halfq] = jb["qT_tok"]
                            else:
                                jb["qT_tok"] = jobs[jidx - 1]["qT_tok"]
                            cc.waitt("pe", jb["qT_tok"])
                            cc.waitt("pe", kv_ready)
                            ncol = 3 + (jidx % 4)
                            nslab = len(slabs)
                            for tk in negm_free.get(ncol, []):
                                cc.waitt("dve", tk)
                            for sj, (k0, w) in enumerate(slabs):
                                bk = k0 // 512
                                cc.waitt("pe", ps_free[bk])
                                cc.ins("pe", lambda e, bk=bk, w=w, k0=k0, sk0=sk0, par=par:
                                       e.matmul(ps4[:, bk * 512: bk * 512 + w],
                                                qTb[:, par * 128:(par + 1) * 128],
                                                kfull[:, sk0 + k0: sk0 + k0 + w],
                                                start=True, stop=True),
                                       inc="tP")
                                smm = cc.tok("tP")
                                is_diag = (si == jb["nseg"] - 1 and sj == nslab - 1)
                                cc.waitt("dve", smm)
                                if is_diag:
                                    cc.waitt("dve", masks_tok)
                                    cc.ins("dve", lambda e, bk=bk, w=w, i=i:
                                           e.tensor_tensor(out=ps4[:, bk * 512: bk * 512 + w],
                                                           in0=ps4[:, bk * 512: bk * 512 + w],
                                                           in1=masks_sb[:, i * 512 + 512 - w:(i + 1) * 512],
                                                           op=ALU.add),
                                           inc="tV")
                                    cc.wait("dve", "tV", cc.state["sems"]["tV"])
                                mcol = ncol if nslab == 1 else 18 + sj
                                cc.wait("dve", "tV", cc.state["sems"]["tV"])
                                cc.ins("dve", lambda e, bk=bk, w=w, mcol=mcol:
                                       e.reduce_max(sm[:, mcol:mcol + 1],
                                                    ps4[:, bk * 512: bk * 512 + w],
                                                    axis=AX.X, negate=True),
                                       inc="tV")
                            qT_free[par] = cc.tok("tP")
                            sc_tok = cc.tok("tV")
                            if nslab > 1:
                                cc.wait("dve", "tV", cc.state["sems"]["tV"])
                                cc.ins("dve", lambda e, ncol=ncol, nslab=nslab:
                                       e.tensor_reduce(sm[:, ncol:ncol + 1], sm[:, 18:18 + nslab],
                                                       axis=AX.X, op=ALU.min),
                                       inc="tV")
                            jb["negm"] = cc.tok("tV")
                            jb["ncol"] = ncol
                            jb["jp"] = jp

                        def stage_R1(jidx):
                            jb = jobs[jidx]
                            si, sw = jb["si"], jb["sw"]
                            jp = jb["jp"]
                            ncol = jb["ncol"]
                            lcol = 7 + (jidx % 4)
                            jb["lcol"] = lcol
                            psl = jp
                            cc.waitt("act", jb["negm"])
                            cc.waitt("act", P_free[psl])
                            for tk in l_free.get(lcol, []):
                                cc.waitt("act", tk)
                            cc.ins("act", lambda e, sw=sw, psl=psl, ncol=ncol, lcol=lcol:
                                   e.activation(out=Pbuf[:, psl * 2048: psl * 2048 + sw],
                                                in_=ps4[:, 0:sw],
                                                func=AF.Exp, bias=sm[:, ncol:ncol + 1],
                                                accum_out=sm[:, lcol:lcol + 1]),
                                   inc="tA")
                            expd = cc.tok("tA")
                            for bk in range((sw + 511) // 512):
                                ps_free[bk] = expd
                            jb["expd"] = expd
                            jb["psl"] = psl
                            jb["lfin"] = expd
                            negm_free[ncol] = [expd]
                        def flush_o():
                            while pending_o:
                                par2, i2, hsl2, o_tok2 = pending_o.pop(0)
                                ptr_t3 = (ptrA, ptrB)[par2]
                                cc.waitt("pe", o_tok2)
                                cc.waitt("pe", ptr_free[par2])
                                cc.ins("pe", lambda e, ptr_t3=ptr_t3, par2=par2:
                                       e.transpose(ptr_t3[:, 0:128], obf[:, par2 * 128:(par2 + 1) * 128],
                                                   idn_b[:]), inc="tP")
                                otr = cc.tok("tP")
                                obf_free[par2] = otr
                                cc.waitt("dve", otr)
                                cc.waitt("dve", ast_free[hsl2])
                                cc.ins("dve", lambda e, hsl2=hsl2, i2=i2, ptr_t3=ptr_t3:
                                       e.tensor_copy(out=astage[:, hsl2 * TOK + i2 * 128: hsl2 * TOK + (i2 + 1) * 128],
                                                     in_=ptr_t3[:, 0:128]),
                                       inc="tV")
                                ptr_free[par2] = cc.tok("tV")

                        def stage_R2(jidx):
                            flush_o()
                            jb = jobs[jidx]
                            i, si, sk0, sw = jb["i"], jb["si"], jb["sk0"], jb["sw"]
                            jp = jb["jp"]
                            par = i % 2
                            ncol = jb["ncol"]
                            accb = ACCB[jp]
                            expd = jb["expd"]
                            psl = jb["psl"]
                            nrounds = (sw + 511) // 512
                            ptevs = []
                            for r in range(nrounds):
                                w = min(512, sw - r * 512)
                                nch = w // 128
                                half2 = r % 2
                                ptr_t2 = (ptrA, ptrB)[half2]
                                cc.waitt("pe", expd)
                                cc.waitt("pe", ptr_free[half2])
                                for j in range(nch):
                                    cc.ins("pe", lambda e, psl=psl, r=r, j=j, ptr_t2=ptr_t2:
                                           e.transpose(ptr_t2[:, j * 128:(j + 1) * 128],
                                                       Pbuf[:, psl * 2048 + r * 512 + j * 128: psl * 2048 + r * 512 + (j + 1) * 128],
                                                       idn_b[:]),
                                           inc="tP" if j == nch - 1 else None)
                                pttr = cc.tok("tP")
                                cc.waitt("dve", pttr)
                                cc.waitt("dve", PT_free[psl * 4 + r])
                                cc.ins("dve", lambda e, psl=psl, r=r, w=w, ptr_t2=ptr_t2:
                                       e.tensor_copy(out=PT[:, (psl * 4 + r) * 512: (psl * 4 + r) * 512 + w],
                                                     in_=ptr_t2[:, 0:w]),
                                       inc="tV")
                                ptev = cc.tok("tV")
                                ptevs.append(ptev)
                                ptr_free[half2] = ptev
                            first = True
                            for r in range(nrounds):
                                w = min(512, sw - r * 512)
                                nch = w // 128
                                cc.waitt("pe", ptevs[r])
                                if first:
                                    cc.waitt("pe", acc_free[jp])
                                for j in range(nch):
                                    kt = (sk0 + r * 512) // 128 + j
                                    cc.ins("pe", lambda e, psl=psl, r=r, j=j, kt=kt, accb=accb,
                                           fst=(first and j == 0),
                                           lst=(r == nrounds - 1 and j == nch - 1):
                                           e.matmul(accb[:, 0:128],
                                                    PT[:, (psl * 4 + r) * 512 + j * 128: (psl * 4 + r) * 512 + (j + 1) * 128],
                                                    v_tm[:, kt * 128:(kt + 1) * 128],
                                                    start=fst, stop=lst),
                                           inc="tP" if (r == nrounds - 1 and j == nch - 1) else None)
                                first = False
                                PT_free[psl * 4 + r] = cc.tok("tP")
                            av_tok = cc.tok("tP")
                            P_free[psl] = av_tok

                            lcol = jb["lcol"]
                            if jb["nseg"] == 1:
                                cc.wait("dve", "tV", cc.state["sems"]["tV"])
                                cc.waitt("dve", jb["lfin"])
                                cc.waitt("dve", o_war[0])
                                cc.ins("dve", lambda e, lcol=lcol:
                                       e.reciprocal(sm[:, 15:16], sm[:, lcol:lcol + 1]), inc="tV")
                                rec = cc.tok("tV")
                                cc.waitt("act", av_tok)
                                cc.waitt("act", rec)
                                cc.waitt("act", obf_free[par])
                                cc.ins("act", lambda e, accb=accb, par=par:
                                       e.activation(out=obf[:, par * 128:(par + 1) * 128],
                                                    in_=accb[:, 0:128],
                                                    func=AF.Copy, scale=sm[:, 15:16]),
                                       inc="tA")
                                o_tok = cc.tok("tA")
                                o_war[0] = o_tok
                                acc_free[jp] = o_tok
                                l_free[lcol] = [rec]
                            else:
                                if si == 0:
                                    cc.waitt("act", av_tok)
                                    cc.waitt("act", jb.get("oacc_war"))
                                    cc.ins("act", lambda e, accb=accb:
                                           e.activation(out=oacc[:], in_=accb[:, 0:128], func=AF.Copy),
                                           inc="tA")
                                    acc_free[jp] = cc.tok("tA")
                                    jb["ncol0"] = ncol
                                    return
                                ncol0 = jobs[jidx - 1]["ncol0"]
                                l0col = jobs[jidx - 1]["lcol"]
                                cc.wait("dve", "tV", cc.state["sems"]["tV"])
                                cc.wait("dve", "tA", cc.state["sems"]["tA"])
                                cc.ins("dve", lambda e, ncol0=ncol0, ncol=ncol:
                                       e.tensor_tensor(out=sm[:, 17:18], in0=sm[:, ncol0:ncol0 + 1],
                                                       in1=sm[:, ncol:ncol + 1], op=ALU.min), inc="tV")
                                cc.wait("dve", "tV", cc.state["sems"]["tV"])
                                cc.ins("dve", lambda e, ncol0=ncol0:
                                       e.tensor_scalar_mul(sm[:, 11:12], sm[:, ncol0:ncol0 + 1], -1.0), inc="tV")
                                cc.ins("dve", lambda e, ncol=ncol:
                                       e.tensor_scalar_mul(sm[:, 12:13], sm[:, ncol:ncol + 1], -1.0), inc="tV")
                                mm_tok = cc.tok("tV")
                                negm_free.setdefault(ncol0, []).append(mm_tok)
                                negm_free.setdefault(ncol, []).append(mm_tok)
                                cc.waitt("act", mm_tok)
                                cc.ins("act", lambda e:
                                       e.activation(out=sm[:, 13:14], in_=sm[:, 17:18],
                                                    func=AF.Exp, bias=sm[:, 11:12]), inc="tA")
                                cc.ins("act", lambda e:
                                       e.activation(out=sm[:, 14:15], in_=sm[:, 17:18],
                                                    func=AF.Exp, bias=sm[:, 12:13]), inc="tA")
                                cexp = cc.tok("tA")
                                cc.waitt("dve", cexp)
                                cc.ins("dve", lambda e, l0col=l0col:
                                       e.tensor_scalar(out=sm[:, l0col:l0col + 1], in0=sm[:, l0col:l0col + 1],
                                                       scalar1=sm[:, 13:14], scalar2=None,
                                                       op0=ALU.mult), inc="tV")
                                cc.wait("dve", "tV", cc.state["sems"]["tV"])
                                cc.ins("dve", lambda e, l0col=l0col, lcol=lcol:
                                       e.scalar_tensor_tensor(out=sm[:, l0col:l0col + 1], in0=sm[:, lcol:lcol + 1],
                                                              scalar=sm[:, 14:15], in1=sm[:, l0col:l0col + 1],
                                                              op0=ALU.mult, op1=ALU.add), inc="tV")
                                cc.ins("dve", lambda e:
                                       e.tensor_scalar(out=oacc[:], in0=oacc[:],
                                                       scalar1=sm[:, 13:14], scalar2=None,
                                                       op0=ALU.mult), inc="tV")
                                cc.wait("dve", "tV", cc.state["sems"]["tV"])
                                cc.waitt("dve", av_tok)
                                cc.ins("dve", lambda e, accb=accb:
                                       e.scalar_tensor_tensor(out=oacc[:], in0=accb[:, 0:128],
                                                              scalar=sm[:, 14:15], in1=oacc[:],
                                                              op0=ALU.mult, op1=ALU.add), inc="tV")
                                comb = cc.tok("tV")
                                acc_free[jp] = comb
                                cc.wait("dve", "tV", cc.state["sems"]["tV"])
                                cc.waitt("dve", o_war[0])
                                cc.ins("dve", lambda e, l0col=l0col:
                                       e.reciprocal(sm[:, 15:16], sm[:, l0col:l0col + 1]), inc="tV")
                                rec = cc.tok("tV")
                                l_free[l0col] = [rec]
                                l_free[lcol] = [rec]
                                cc.waitt("act", rec)
                                cc.waitt("act", obf_free[par])
                                cc.ins("act", lambda e, par=par:
                                       e.activation(out=obf[:, par * 128:(par + 1) * 128],
                                                    in_=oacc[:],
                                                    func=AF.Copy, scale=sm[:, 15:16]),
                                       inc="tA")
                                o_tok = cc.tok("tA")
                                o_war[0] = o_tok
                            ow = cc.tok("tV")
                            for jj in range(jidx + 1, len(jobs)):
                                if jobs[jj]["si"] == 0 and jobs[jj]["nseg"] > 1:
                                    jobs[jj]["oacc_war"] = ow
                                    break
                            pending_o.append((par, i, hsl, o_tok))

                        for jidx in range(len(jobs)):
                            stage_S(jidx)
                            stage_R1(jidx)
                            if jidx > 0:
                                stage_R2(jidx - 1)
                        stage_R2(len(jobs) - 1)
                        flush_o()

                        kfull_free = cc.tok("tP")
                        vtm_free = cc.tok("tP")
                        qh_free[hsl] = cc.tok("tP")
                        fin = cc.tok("tV")
                        cc.waitt("gp", fin)
                        da = "dA0" if hsl == 0 else "dA1"
                        cc.ins("gp", lambda e, h=h, hsl=hsl:
                               e.dma_start(out=attn_a[h], in_=astage[:, hsl * TOK:(hsl + 1) * TOK]),
                               inc=da, k=16)
                        ast_free[hsl] = cc.tok(da)

                c.run_phase(ph_b)
            c.full_gate()

        for hf in range(2):
            with ExitStack() as stCD:
                x2 = stCD.enter_context(nc.sbuf_tensor(f"x2_{hf}", [128, DC * TH], F32))

                with ExitStack() as stC:
                    wo_bf = stC.enter_context(nc.sbuf_tensor(f"wo_bf{hf}", [128, (H * VD // 128) * D], BF16))
                    wstC = stC.enter_context(nc.sbuf_tensor(f"wstC{hf}", [128, 2 * 2048], F32))
                    attn_res = stC.enter_context(nc.sbuf_tensor(f"attn_res{hf}", [128, H * TH], BF16))
                    xringC = stC.enter_context(nc.sbuf_tensor(f"xringC{hf}", [128, 2 * D], F32))

                    ICC = H * VD // 128

                    def ph_c(cc, hf=hf):
                        xr_free = [None, None]
                        bank_free = [None, None]
                        ps_free = [None, None]
                        for ii in range(NTH):
                            i = hf * NTH + ii
                            sl = ii % 2
                            dx = "dX0" if sl == 0 else "dX1"
                            cc.waitt("gp", xr_free[sl])
                            cc.ins("gp", lambda e, i=i, sl=sl:
                                   e.dma_start(out=xringC[:, sl * D:(sl + 1) * D], in_=xa[2 * i]),
                                   inc=dx, k=16)
                            xl = cc.tok(dx)
                            cc.waitt("pe", xl)
                            for r in range(DC // 4):
                                pb = (pex, pvb)[r % 2]
                                cc.waitt("pe", bank_free[r % 2])
                                for j in range(4):
                                    dcx = r * 4 + j
                                    cc.ins("pe", lambda e, sl=sl, dcx=dcx, j=j, pb=pb:
                                           e.transpose(pb[:, j * 128:(j + 1) * 128],
                                                       xringC[:, sl * D + dcx * 128: sl * D + (dcx + 1) * 128],
                                                       idn_f[:]),
                                           inc="tP" if j == 3 else None)
                                tr = cc.tok("tP")
                                cc.waitt("dve", tr)
                                for j in range(4):
                                    dcx = r * 4 + j
                                    cc.ins("dve", lambda e, dcx=dcx, ii=ii, j=j, pb=pb:
                                           e.tensor_copy(out=x2[:, dcx * TH + ii * 128: dcx * TH + (ii + 1) * 128],
                                                         in_=pb[:, j * 128:(j + 1) * 128]),
                                           inc="tV" if j == 3 else None)
                                bank_free[r % 2] = cc.tok("tV")
                            xr_free[sl] = cc.tok("tV")
                        x2_ready = cc.tok("tV")

                        cast_tok = [None] * ICC
                        for icx in range(ICC):
                            slot = icx % 2
                            if icx >= 2:
                                cc.waitt("sp", cast_tok[icx - 2])
                            dw = "dW0" if slot == 0 else "dW1"
                            cc.ins("sp", lambda e, icx=icx, slot=slot:
                                   e.dma_start(out=wstC[:, slot * 2048: slot * 2048 + D],
                                               in_=wo_d.ap()[icx * 128:(icx + 1) * 128, :]),
                                   inc=dw, k=16)
                            cc.wait("dve", dw, cc.state["sems"][dw])
                            cast_tok[icx] = cc.ins("dve", lambda e, icx=icx, slot=slot:
                                                   e.tensor_copy(out=wo_bf[:, icx * D:(icx + 1) * D],
                                                                 in_=wstC[:, slot * 2048: slot * 2048 + D]),
                                                   inc="tV")
                        wo_ready = cast_tok[-1]

                        for icx in range(ICC):
                            cc.ins("sp", lambda e, icx=icx, hf=hf:
                                   e.dma_start(out=attn_res[:, icx * TH:(icx + 1) * TH],
                                               in_=attn_a[icx][:, hf * TH:(hf + 1) * TH]),
                                   inc="dS", k=16)
                        attn_tok = cc.tok("dS")

                        nsl = (TH + 511) // 512
                        for oc in range(DC):
                            pbk = oc % 2
                            cc.waitt("pe", ps_free[pbk])
                            cc.waitt("pe", wo_ready)
                            cc.waitt("pe", attn_tok)
                            for icx in range(ICC):
                                for s in range(nsl):
                                    w = min(512, TH - s * 512)
                                    cc.ins("pe", lambda e, pbk=pbk, icx=icx, s=s, w=w, oc=oc,
                                           first=(icx == 0), last=(icx == ICC - 1):
                                           e.matmul(ps4[:, pbk * 1024 + s * 512: pbk * 1024 + s * 512 + w],
                                                    wo_bf[:, icx * D + oc * 128: icx * D + oc * 128 + 128],
                                                    attn_res[:, icx * TH + s * 512: icx * TH + s * 512 + w],
                                                    start=first, stop=last),
                                           inc="tP" if (icx == ICC - 1 and s == nsl - 1) else None)
                            grp = cc.tok("tP")
                            cc.waitt("dve", grp)
                            cc.waitt("dve", x2_ready)
                            cc.ins("dve", lambda e, pbk=pbk, oc=oc:
                                   e.tensor_tensor(out=x2[:, oc * TH:(oc + 1) * TH],
                                                   in0=ps4[:, pbk * 1024: pbk * 1024 + TH],
                                                   in1=x2[:, oc * TH:(oc + 1) * TH], op=ALU.add),
                                   inc="tV")
                            ps_free[pbk] = cc.tok("tV")

                    c.run_phase(ph_c)
                c.full_gate()

                with ExitStack() as stD:
                    xn2 = stD.enter_context(nc.sbuf_tensor(f"xn2_{hf}", [128, DC * TH], BF16))
                    gated = stD.enter_context(nc.sbuf_tensor(f"gated{hf}", [128, SUPC * TH], BF16))
                    wg_bf = stD.enter_context(nc.sbuf_tensor(f"wg_bf{hf}", [128, 2 * DC * SUP], BF16))
                    wu_bf = stD.enter_context(nc.sbuf_tensor(f"wu_bf{hf}", [128, 2 * DC * SUP], BF16))
                    wd_bf = stD.enter_context(nc.sbuf_tensor(f"wd_bf{hf}", [128, SUPC * D], BF16))
                    WSLOT = max(D, 4 * SUP)
                    wtmp_cols = max(2 * WSLOT, 3 * TH)
                    wtmp = stD.enter_context(nc.sbuf_tensor(f"wtmp{hf}", [128, wtmp_cols], F32))
                    silu_t = stD.enter_context(nc.sbuf_tensor(f"silu{hf}", [128, max(2 * 512, TH)], F32))

                    def ph_d(cc, hf=hf):
                        nsl = (TH + 511) // 512
                        rows = wtmp
                        for dcx in range(DC):
                            cc.ins("gp", lambda e, dcx=dcx:
                                   e.tensor_copy(out=xn2[:, dcx * TH:(dcx + 1) * TH],
                                                 in_=x2[:, dcx * TH:(dcx + 1) * TH]),
                                   inc="tG" if dcx == DC - 1 else None)
                        x2b = cc.tok("tG")
                        cc.waitt("pe", x2b)
                        for s in range(nsl):
                            w = min(512, TH - s * 512)
                            for dcx in range(DC):
                                cc.ins("pe", lambda e, s=s, w=w, dcx=dcx,
                                       first=(dcx == 0), last=(dcx == DC - 1):
                                       e.matmul(ps4[0:1, s * 512: s * 512 + w],
                                                ones_neg[:],
                                                xn2[:, dcx * TH + s * 512: dcx * TH + s * 512 + w],
                                                start=first, stop=last),
                                       inc="tP" if dcx == DC - 1 else None)
                        sneg = cc.tok("tP")
                        cc.waitt("gp", sneg)
                        for dcx in range(DC):
                            cc.ins("gp", lambda e, dcx=dcx:
                                   e.tensor_tensor(out=xn2[:, dcx * TH:(dcx + 1) * TH],
                                                   in0=xn2[:, dcx * TH:(dcx + 1) * TH],
                                                   in1=xn2[:, dcx * TH:(dcx + 1) * TH], op=ALU.mult),
                                   inc="tG" if dcx == DC - 1 else None)
                        sq_done = cc.tok("tG")
                        cc.waitt("pe", sq_done)
                        for s in range(nsl):
                            w = min(512, TH - s * 512)
                            for dcx in range(DC):
                                cc.ins("pe", lambda e, s=s, w=w, dcx=dcx,
                                       first=(dcx == 0), last=(dcx == DC - 1):
                                       e.matmul(ps4[0:1, 1024 + s * 512: 1024 + s * 512 + w],
                                                ones_inv[:],
                                                xn2[:, dcx * TH + s * 512: dcx * TH + s * 512 + w],
                                                start=first, stop=last),
                                       inc="tP" if dcx == DC - 1 else None)
                        ssq = cc.tok("tP")
                        cc.waitt("dve", sneg)
                        cc.ins("dve", lambda e: e.tensor_copy(out=rows[0:1, 0:TH], in_=ps4[0:1, 0:TH]), inc="tV")
                        cc.waitt("dve", ssq)
                        cc.ins("dve", lambda e: e.tensor_copy(out=rows[0:1, TH:2 * TH], in_=ps4[0:1, 1024:1024 + TH]), inc="tV")
                        cc.wait("dve", "tV", cc.state["sems"]["tV"])
                        cc.ins("dve", lambda e:
                               e.tensor_tensor(out=rows[0:1, 2 * TH:3 * TH], in0=rows[0:1, 0:TH],
                                               in1=rows[0:1, 0:TH], op=ALU.mult), inc="tV")
                        cc.wait("dve", "tV", cc.state["sems"]["tV"])
                        cc.ins("dve", lambda e:
                               e.tensor_tensor(out=rows[0:1, TH:2 * TH], in0=rows[0:1, TH:2 * TH],
                                               in1=rows[0:1, 2 * TH:3 * TH], op=ALU.subtract), inc="tV")
                        var_tok = cc.tok("tV")
                        cc.waitt("act", var_tok)
                        cc.ins("act", lambda e:
                               e.activation(out=rows[0:1, 2 * TH:3 * TH], in_=rows[0:1, TH:2 * TH],
                                            func=AF.Ln, bias=epsb[0:1, 0:1]), inc="tA")
                        cc.wait("act", "tA", cc.state["sems"]["tA"])
                        cc.ins("act", lambda e:
                               e.activation(out=rows[0:1, 2 * TH:3 * TH], in_=rows[0:1, 2 * TH:3 * TH],
                                            func=AF.Exp, scale=-0.5), inc="tA")
                        rstd_tok = cc.tok("tA")
                        cc.waitt("pe", rstd_tok)
                        for s in range(nsl):
                            w = min(512, TH - s * 512)
                            cc.ins("pe", lambda e, s=s, w=w:
                                   e.matmul(ps4[:, s * 512: s * 512 + w], ones_row[:],
                                            rows[0:1, s * 512: s * 512 + w], start=True, stop=True),
                                   inc="tP")
                            cc.ins("pe", lambda e, s=s, w=w:
                                   e.matmul(ps4[:, 1024 + s * 512: 1024 + s * 512 + w], ones_row[:],
                                            rows[0:1, 2 * TH + s * 512: 2 * TH + s * 512 + w],
                                            start=True, stop=True),
                                   inc="tP")
                        bc_tok = cc.tok("tP")
                        cc.waitt("dve", bc_tok)
                        for dcx in range(DC):
                            cc.ins("dve", lambda e, dcx=dcx:
                                   e.tensor_tensor(out=silu_t[:, 0:TH], in0=x2[:, dcx * TH:(dcx + 1) * TH],
                                                   in1=ps4[:, 0:TH], op=ALU.add), inc="tV")
                            cc.wait("dve", "tV", cc.state["sems"]["tV"])
                            cc.ins("dve", lambda e, dcx=dcx:
                                   e.tensor_tensor(out=xn2[:, dcx * TH:(dcx + 1) * TH],
                                                   in0=silu_t[:, 0:TH], in1=ps4[:, 1024:1024 + TH],
                                                   op=ALU.mult), inc="tV")
                            cc.wait("dve", "tV", cc.state["sems"]["tV"])
                        xn2_tok = cc.tok("tV")

                        DGRP = DC // 4 if DC % 4 == 0 else None
                        st_free = [bc_tok, bc_tok]
                        pj = [0]
                        gu_tok = [None, None]
                        wd_tok_box = [None]
                        wgu_free = [None, None]
                        wd_free = [None]

                        def load_gu(s):
                            sb = s % 2
                            toks = []
                            for wsrc, wdst in ((wgT_d, wg_bf), (wuT_d, wu_bf)):
                                for g in range(DC // 4):
                                    slot = pj[0] % 2
                                    pj[0] += 1
                                    dw = "dW0" if slot == 0 else "dW1"
                                    cc.waitt("sp", st_free[slot])
                                    src = bass.AP(wsrc, (4 * g * 128) * F + s * SUP,
                                                  [[F, 128], [128 * F, 4], [1, SUP]])
                                    dst_ap = bass.AP(wtmp, slot * WSLOT,
                                                     [[wtmp_cols, 128], [SUP, 4], [1, SUP]])
                                    cc.ins("sp", lambda e, src=src, dst_ap=dst_ap:
                                           e.dma_start(out=dst_ap, in_=src), inc=dw, k=16)
                                    cc.wait("dve", dw, cc.state["sems"][dw])
                                    cc.waitt("dve", wgu_free[sb])
                                    tokc = cc.ins("dve", lambda e, slot=slot, wdst=wdst, sb=sb, g=g:
                                                  e.tensor_copy(out=wdst[:, sb * DC * SUP + 4 * g * SUP: sb * DC * SUP + 4 * (g + 1) * SUP],
                                                                in_=wtmp[:, slot * WSLOT: slot * WSLOT + 4 * SUP]),
                                                  inc="tV")
                                    st_free[slot] = tokc
                                    toks.append(tokc)
                            gu_tok[sb] = toks[-1]

                        def load_wd(s):
                            toks = []
                            for hcx in range(SUPC):
                                slot = pj[0] % 2
                                pj[0] += 1
                                dw = "dW0" if slot == 0 else "dW1"
                                cc.waitt("sp", st_free[slot])
                                src = wdT_d.ap()[s * SUP + hcx * 128: s * SUP + (hcx + 1) * 128, :]
                                cc.ins("sp", lambda e, src=src, slot=slot:
                                       e.dma_start(out=wtmp[:, slot * WSLOT: slot * WSLOT + D], in_=src),
                                       inc=dw, k=16)
                                cc.wait("dve", dw, cc.state["sems"][dw])
                                cc.waitt("dve", wd_free[0])
                                tokc = cc.ins("dve", lambda e, slot=slot, hcx=hcx:
                                              e.tensor_copy(out=wd_bf[:, hcx * D:(hcx + 1) * D],
                                                            in_=wtmp[:, slot * WSLOT: slot * WSLOT + D]),
                                              inc="tV")
                                st_free[slot] = tokc
                                toks.append(tokc)
                            wd_tok_box[0] = toks[-1]

                        silu_free = [None, None]
                        gbank = [None, None, None, None]
                        dpsf = [None, None]
                        gated_free = None

                        load_gu(0)
                        load_wd(0)
                        for sp_i in range(NSUP):
                            sb = sp_i % 2
                            wd_tok = wd_tok_box[0]
                            if sp_i + 1 < NSUP:
                                load_gu(sp_i + 1)
                            for hcx in range(SUPC):
                                for s in range(nsl):
                                    w = min(512, TH - s * 512)
                                    gb = s % 2
                                    ub = 2 + s % 2
                                    cc.waitt("pe", gbank[gb])
                                    cc.waitt("pe", gbank[ub])
                                    cc.waitt("pe", gu_tok[sb])
                                    cc.waitt("pe", xn2_tok)
                                    for dcx in range(DC):
                                        cc.ins("pe", lambda e, gb=gb, w=w, s=s, dcx=dcx, hcx=hcx, sb=sb,
                                               first=(dcx == 0), last=(dcx == DC - 1):
                                               e.matmul(ps4[:, gb * 512: gb * 512 + w],
                                                        wg_bf[:, sb * DC * SUP + dcx * SUP + hcx * 128: sb * DC * SUP + dcx * SUP + (hcx + 1) * 128],
                                                        xn2[:, dcx * TH + s * 512: dcx * TH + s * 512 + w],
                                                        start=first, stop=last),
                                               inc="tP" if dcx == DC - 1 else None)
                                    g_grp = cc.tok("tP")
                                    for dcx in range(DC):
                                        cc.ins("pe", lambda e, ub=ub, w=w, s=s, dcx=dcx, hcx=hcx, sb=sb,
                                               first=(dcx == 0), last=(dcx == DC - 1):
                                               e.matmul(ps4[:, ub * 512: ub * 512 + w],
                                                        wu_bf[:, sb * DC * SUP + dcx * SUP + hcx * 128: sb * DC * SUP + dcx * SUP + (hcx + 1) * 128],
                                                        xn2[:, dcx * TH + s * 512: dcx * TH + s * 512 + w],
                                                        start=first, stop=last),
                                               inc="tP" if dcx == DC - 1 else None)
                                    u_grp = cc.tok("tP")
                                    cc.waitt("act", g_grp)
                                    cc.waitt("act", silu_free[s % 2])
                                    cc.ins("act", lambda e, gb=gb, w=w, s=s:
                                           e.activation(out=silu_t[:, (s % 2) * 512:(s % 2) * 512 + w],
                                                        in_=ps4[:, gb * 512: gb * 512 + w],
                                                        func=AF.Silu), inc="tA")
                                    sil = cc.tok("tA")
                                    gbank[gb] = sil
                                    cc.waitt("dve", sil)
                                    cc.waitt("dve", u_grp)
                                    if sp_i > 0 or hcx > 0 or s > 0:
                                        cc.waitt("dve", gated_free)
                                    cc.ins("dve", lambda e, ub=ub, w=w, hcx=hcx, s=s:
                                           e.tensor_tensor(out=gated[:, hcx * TH + s * 512: hcx * TH + s * 512 + w],
                                                           in0=silu_t[:, (s % 2) * 512:(s % 2) * 512 + w],
                                                           in1=ps4[:, ub * 512: ub * 512 + w], op=ALU.mult),
                                           inc="tV")
                                    gbank[ub] = cc.tok("tV")
                                    silu_free[s % 2] = gbank[ub]
                            gated_tok = cc.tok("tV")
                            wgu_free[sb] = cc.tok("tP")
                            cc.waitt("pe", gated_tok)
                            cc.waitt("pe", wd_tok)
                            for dcx in range(DC):
                                for s in range(nsl):
                                    w = min(512, TH - s * 512)
                                    pbi = (dcx * nsl + s) % 2
                                    pb = (pex, pvb)[pbi]
                                    cc.waitt("pe", dpsf[pbi])
                                    for hcx in range(SUPC):
                                        cc.ins("pe", lambda e, pb=pb, w=w, dcx=dcx, s=s, hcx=hcx,
                                               first=(hcx == 0), last=(hcx == SUPC - 1):
                                               e.matmul(pb[:, 0:w],
                                                        wd_bf[:, hcx * D + dcx * 128: hcx * D + (dcx + 1) * 128],
                                                        gated[:, hcx * TH + s * 512: hcx * TH + s * 512 + w],
                                                        start=first, stop=last),
                                               inc="tP" if hcx == SUPC - 1 else None)
                                    dgrp = cc.tok("tP")
                                    cc.waitt("dve", dgrp)
                                    cc.ins("dve", lambda e, pb=pb, w=w, dcx=dcx, s=s:
                                           e.tensor_tensor(out=x2[:, dcx * TH + s * 512: dcx * TH + s * 512 + w],
                                                           in0=pb[:, 0:w],
                                                           in1=x2[:, dcx * TH + s * 512: dcx * TH + s * 512 + w],
                                                           op=ALU.add),
                                           inc="tV")
                                    dpsf[pbi] = cc.tok("tV")
                            gated_free = cc.tok("tP")
                            wd_free[0] = cc.tok("tP")
                            if sp_i + 1 < NSUP:
                                load_wd(sp_i + 1)
                        ffn_done = cc.tok("tV")

                        ost_free = [st_free[0], st_free[1]]
                        bank_free2 = [None, None]
                        for ii in range(NTH):
                            i = hf * NTH + ii
                            osl = ii % 2
                            cc.waitt("pe", ffn_done)
                            for r in range(DC // 4):
                                pb = (pex, pvb)[r % 2]
                                cc.waitt("pe", bank_free2[r % 2])
                                for j in range(4):
                                    dcx = r * 4 + j
                                    cc.ins("pe", lambda e, dcx=dcx, ii=ii, j=j, pb=pb:
                                           e.transpose(pb[:, j * 128:(j + 1) * 128],
                                                       x2[:, dcx * TH + ii * 128: dcx * TH + (ii + 1) * 128],
                                                       idn_f[:]),
                                           inc="tP" if j == 3 else None)
                                tr = cc.tok("tP")
                                cc.waitt("dve", tr)
                                cc.waitt("dve", ost_free[osl])
                                cc.ins("dve", lambda e, r=r, osl=osl, pb=pb:
                                       e.tensor_copy(out=wtmp[:, osl * WSLOT + r * 512: osl * WSLOT + (r + 1) * 512],
                                                     in_=pb[:, 0:512]),
                                       inc="tV")
                                bank_free2[r % 2] = cc.tok("tV")
                            ev = cc.tok("tV")
                            cc.waitt("sp", ev)
                            dy = "dY0" if osl == 0 else "dY1"
                            cc.ins("sp", lambda e, i=i, osl=osl:
                                   e.dma_start(out=ya[i], in_=wtmp[:, osl * WSLOT: osl * WSLOT + D]),
                                   inc=dy, k=16)
                            ost_free[osl] = cc.tok(dy)

                    c.run_phase(ph_d)
                c.full_gate()

        def ph_fin(cc):
            for e in ENGINES:
                for snm in list(cc.state["sems"].keys()):
                    if snm.startswith("d"):
                        cc.wait(e, snm, cc.state["sems"][snm])

        c.run_phase(ph_fin)

    return nc



def rope_tables(S_, dtype=np.float32):
    half = ROPE // 2
    inv = 1.0 / (10000.0 ** (np.arange(half, dtype=np.float64) * 2.0 / ROPE))
    fr = np.arange(S_, dtype=np.float64)[:, None] * inv[None, :]
    cos = np.cos(fr).astype(np.float32)
    sin = np.sin(fr).astype(np.float32)
    cosI = np.repeat(cos, 2, axis=1)
    sinI = np.empty((S_, ROPE), np.float32)
    sinI[:, 0::2] = -sin
    sinI[:, 1::2] = sin
    return cosI, sinI


def host_core_inputs(cfg, xb, weights, p):
    S, D = cfg["S"], cfg["D"]
    NT, NO = S // 128, S // 256
    gl = np.empty(NT, np.int64)
    gl[0::2] = 2 * np.arange(NO) + p
    gl[1::2] = 2 * np.arange(NO) + (1 - p)
    xp = np.ascontiguousarray(xb.reshape(NT, 128, D)[gl].reshape(S, D))

    cosI, sinI = rope_tables(S)
    ck = cosI.reshape(NT, 128, ROPE)[gl]
    sk = sinI.reshape(NT, 128, ROPE)[gl]
    SCALE = float(HD) ** -0.5
    ck_sb = np.ascontiguousarray((ck * SCALE).transpose(1, 0, 2).reshape(128, NT * ROPE))
    sk_sb = np.ascontiguousarray((sk * SCALE).transpose(1, 0, 2).reshape(128, NT * ROPE))
    own = 2 * np.arange(NO) + p
    cq = cosI.reshape(NT, 128, ROPE)[own]
    sq = sinI.reshape(NT, 128, ROPE)[own]
    cq_sb = np.ascontiguousarray(cq.transpose(1, 0, 2).reshape(128, NO * ROPE))
    sq_sb = np.ascontiguousarray(sq.transpose(1, 0, 2).reshape(128, NO * ROPE))

    masks = np.zeros((NO, 128, 512), np.float32)
    for i in range(NO):
        E = 256 * (i + 1)
        w = 512 if E % 512 == 0 else E % 512
        q_glob = 128 * (2 * i + p) + np.arange(128)
        kslot = np.arange(E - w, E)
        ktile = kslot // 128
        k_glob = 128 * gl[ktile] + (kslot % 128)
        m = np.where(k_glob[None, :] > q_glob[:, None], MASK_NEG, 0.0)
        masks[i, :, 512 - w:] = m
        masks[i, :, :512 - w] = 0.0
    masks_sb = np.ascontiguousarray(masks.transpose(1, 0, 2).reshape(128, NO * 512)).astype(ml_dtypes.bfloat16)

    d = dict(weights)
    d["x"] = xp
    d["cosq"] = cq_sb
    d["sinq"] = sq_sb
    d["cosk"] = ck_sb
    d["sink"] = sk_sb
    d["masks"] = masks_sb
    d["idnf"] = np.eye(128, dtype=np.float32)
    d["idnb"] = np.eye(128, dtype=np.float32).astype(ml_dtypes.bfloat16)
    return d, gl, own


_prog_cache = {}


def kernel(L0_attn_hidden_states, L0_attn_Wqa, L0_attn_Wqb, L0_attn_Wkva,
           L0_attn_Wkvb, L0_attn_Wo, L0_ffn_w_gate, L0_ffn_w_up, L0_ffn_w_down):
    cfg = CFG_FULL
    S, D = cfg["S"], cfg["D"]
    B = L0_attn_hidden_states.shape[0]
    NO = S // 256

    key = "full"
    if key not in _prog_cache:
        _prog_cache[key] = build_program(cfg)
    nc = _prog_cache[key]

    weights = {
        "wqa": np.ascontiguousarray(L0_attn_Wqa, np.float32),
        "wqb": np.ascontiguousarray(L0_attn_Wqb, np.float32),
        "wkva": np.ascontiguousarray(L0_attn_Wkva, np.float32),
        "wkvb": np.ascontiguousarray(L0_attn_Wkvb, np.float32),
        "wo": np.ascontiguousarray(L0_attn_Wo, np.float32),
        "wgT": np.ascontiguousarray(np.asarray(L0_ffn_w_gate, np.float32).T),
        "wuT": np.ascontiguousarray(np.asarray(L0_ffn_w_up, np.float32).T),
        "wdT": np.ascontiguousarray(np.asarray(L0_ffn_w_down, np.float32).T),
    }
    x_full = np.asarray(L0_attn_hidden_states, np.float32)

    in_maps = []
    owns = []
    for core in range(8):
        b, p = core // 2, core % 2
        d, gl, own = host_core_inputs(cfg, x_full[b], weights, p)
        in_maps.append(d)
        owns.append(own)

    res = run_bass_kernel_spmd(nc, in_maps, core_ids=list(range(8)))

    out = np.empty((B, S, D), np.float32)
    for core in range(8):
        b, p = core // 2, core % 2
        y = res.results[core]["y"].reshape(NO, 128, D)
        out[b].reshape(S // 128, 128, D)[owns[core]] = y
    return out
